# version 01 (final)
# baseline (speedup 1.0000x reference)
import numpy as np
import ml_dtypes
from contextlib import ExitStack

import concourse.bass as bass
import concourse.mybir as mybir
from concourse.bass_utils import run_bass_kernel_spmd

F32 = mybir.dt.float32
BF16 = mybir.dt.bfloat16
ALU = mybir.AluOpType
AF = mybir.ActivationFunctionType
AX = mybir.AxisListType

NC = 8
D = 1024
B = 2
TG = 8192
TB = 1024
TL = 2 * TB
EPS = 1e-5
BF = ml_dtypes.bfloat16



def fm_slice(x_btd, c):
    sl = np.concatenate([x_btd[b, c * TB:(c + 1) * TB, :].T for b in range(B)], axis=1)
    return np.ascontiguousarray(sl.astype(np.float32, copy=False))


def unfm(tiles):
    out = np.empty((B, TG, D), np.float32)
    for c, t in enumerate(tiles):
        for b in range(B):
            out[b, c * TB:(c + 1) * TB, :] = np.asarray(t)[:, b * TB:(b + 1) * TB].T
    return out


def vcols(v):
    v = np.asarray(v, np.float32).reshape(-1)
    k = (v.shape[0] + 127) // 128
    out = np.zeros((128, k), np.float32)
    for i in range(k):
        seg = v[i * 128:(i + 1) * 128]
        out[:len(seg), i] = seg
    return np.ascontiguousarray(out)


def hT(w):
    return np.ascontiguousarray(np.asarray(w, np.float32).T)


IDENT128 = np.eye(128, dtype=BF)


class Phase:
    def __init__(self, name):
        self.name = name
        self.nc = bass.Bass("TRN2", target_bir_lowering=False)
        self.stack = ExitStack()
        self._sems = {}
        self._dma_n = {}

    def din(self, name, shape, dtype=F32):
        return self.nc.dram_tensor(name, list(shape), dtype, kind="ExternalInput").ap()

    def dout(self, name, shape, dtype=F32):
        return self.nc.dram_tensor(name, list(shape), dtype, kind="ExternalOutput").ap()

    def sb(self, name, shape, dtype=F32, scope=None, tdest=False):
        st = scope if scope is not None else self.stack
        return st.enter_context(self.nc.sbuf_tensor(name, list(shape), dtype, is_transpose_dest=tdest))

    def scope(self):
        return ExitStack()

    _psn = 0

    def ps(self, shape, dtype=F32, name=None, scope=None):
        if name is None:
            Phase._psn += 1
            name = f"ps{Phase._psn}"
        st = scope if scope is not None else self.stack
        return st.enter_context(self.nc.psum_tensor(name + self.name, list(shape), dtype))

    def sem(self, name):
        if name not in self._sems:
            self._sems[name] = [self.stack.enter_context(self.nc.semaphore(name)), 0]
        return self._sems[name]

    def dma(self, eng, dst, src, q="ld"):
        s = self.sem(q)
        eng.dma_start(dst, src).then_inc(s[0], 16)
        s[1] += 16

    def dma_wait_all(self, eng, q="ld"):
        s = self.sem(q)
        eng.wait_ge(s[0], s[1])

    def done(self):
        self.stack.close()
        return self



def emit_ln_apply_sized(p, xt, nT, apply_cb, name, scope=None):
    return emit_ln_apply(p, xt, nT, apply_cb, name, scope=scope)


def emit_ln_apply(p, xt, nT, apply_cb, name, scope=None):
    nc = p.nc
    NCH = nT // 512
    xsq = p.sb(name + "_xsq", [128, 8 * 512], scope=scope)
    stats = p.sb(name + "_stats", [1, 2 * nT], scope=scope)
    onesK = p.sb(name + "_onesK", [128, 1], scope=scope)
    ones1 = p.sb(name + "_ones1", [1, 128], scope=scope)
    musq = p.sb(name + "_musq", [1, 512], scope=scope)
    psS = p.ps([1, 512], scope=scope)
    psQ = p.ps([1, 512], scope=scope)
    psMu = p.ps([128, 512], scope=scope)
    psRs = p.ps([128, 512], scope=scope)

    with nc.Block() as blk:
        @blk.gpsimd
        def _(g):
            g.memset(onesK[:], 1.0)
            g.memset(ones1[:], 1.0)

    for c in range(NCH):
        with nc.Block() as blk:
            @blk.scalar
            def _(s, c=c):
                for k in range(8):
                    s.activation(xsq[:, k * 512:(k + 1) * 512],
                                 xt[:, k * nT + c * 512:k * nT + c * 512 + 512],
                                 AF.Square)
        with nc.Block() as blk:
            @blk.tensor
            def _(t, c=c):
                for k in range(8):
                    t.matmul(psS[:], onesK[:], xt[:, k * nT + c * 512:k * nT + c * 512 + 512],
                             start=(k == 0), stop=(k == 7))
                for k in range(8):
                    t.matmul(psQ[:], onesK[:], xsq[:, k * 512:(k + 1) * 512],
                             start=(k == 0), stop=(k == 7))
        with nc.Block() as blk:
            @blk.vector
            def _(v, c=c):
                mu = stats[0:1, c * 512:(c + 1) * 512]
                va = stats[0:1, nT + c * 512:nT + (c + 1) * 512]
                v.tensor_single_scalar(mu, psS[0:1, :], 1.0 / D, ALU.mult)
                v.tensor_single_scalar(va, psQ[0:1, :], 1.0 / D, ALU.mult)
                v.tensor_mul(musq[0:1, :], mu, mu)
                v.tensor_sub(va, va, musq[0:1, :])
                v.tensor_single_scalar(va, va, EPS, ALU.add)
        with nc.Block() as blk:
            @blk.scalar
            def _(s, c=c):
                va = stats[0:1, nT + c * 512:nT + (c + 1) * 512]
                s.activation(va, va, AF.Sqrt)
        with nc.Block() as blk:
            @blk.vector
            def _(v, c=c):
                va = stats[0:1, nT + c * 512:nT + (c + 1) * 512]
                v.reciprocal(va, va)
        with nc.Block() as blk:
            @blk.tensor
            def _(t, c=c):
                mu = stats[0:1, c * 512:(c + 1) * 512]
                va = stats[0:1, nT + c * 512:nT + (c + 1) * 512]
                t.matmul(psMu[:], ones1[:], mu, start=True, stop=True)
                t.matmul(psRs[:], ones1[:], va, start=True, stop=True)
        with nc.Block() as blk:
            engines = {}

            @blk.vector
            def _(v, c=c):
                apply_cb_v = apply_cb.get('v')
                if apply_cb_v:
                    apply_cb_v(v, c, psMu, psRs)

            @blk.scalar
            def _(s, c=c):
                f = apply_cb.get('s')
                if f:
                    f(s, c, psMu, psRs)

            @blk.sync
            def _(sy, c=c):
                f = apply_cb.get('sync')
                if f:
                    f(sy, c, psMu, psRs)


def emit_cast_load(p, eng_sync, dram, sb_stage, sb_out, rows, cols, name):
    raise NotImplementedError



def build_phase1():
    p = Phase("ph1")
    nc = p.nc
    x_in = p.din("x_fm", [D, TL])
    w_h = p.din("W_hidden", [D, 4096])
    b_h = p.din("b_hidden", [128, 32])
    w_qk = p.din("W_qk", [D, 2048])
    b_qk = p.din("b_qk", [128, 16])
    lng = p.din("lng", [128, 8])
    lnb = p.din("lnb", [128, 8])
    ident_in = p.din("ident_i", [128, 128], BF16)

    xplus_o = p.dout("xplus_fm", [D, TL])
    qkT_o = p.dout("qkT", [TL, 2048], BF16)
    vg_o = p.dout("vg", [2048, TL], BF16)
    scol_o = p.dout("scol", [2048, 2])

    xnb = p.sb("xnb", [128, 8 * TL], BF16)
    lngc = p.sb("lngc", [128, 8])
    lnbc = p.sb("lnbc", [128, 8])
    bhc = p.sb("bhc", [128, 32])
    bqkc = p.sb("bqkc", [128, 16])
    ident = p.sb("ident", [128, 128], BF16)

    with p.scope() as sc:
        xt = p.sb("xt", [128, 8 * TL], scope=sc)
        xpl = p.sb("xpl", [128, 512], scope=sc)

        with nc.Block() as blk:
            @blk.sync
            def _(s):
                for k in range(8):
                    p.dma(s, xt[:, k * TL:(k + 1) * TL], x_in[k * 128:(k + 1) * 128, :])
                p.dma(s, lngc[:], lng)
                p.dma(s, lnbc[:], lnb)
                p.dma(s, bhc[:], b_h)
                p.dma(s, bqkc[:], b_qk)
                p.dma(s, ident[:], ident_in)
                p.dma_wait_all(s)

        def apply_v(v, c, psMu, psRs):
            for k in range(8):
                xs = xt[:, k * TL + c * 512:k * TL + c * 512 + 512]
                xn = xnb[:, k * TL + c * 512:k * TL + c * 512 + 512]
                v.tensor_tensor(xpl[:], xs, psMu[:], ALU.subtract)
                v.tensor_tensor(xpl[:], xpl[:], psRs[:], ALU.mult)
                v.tensor_tensor(xs, xs, xpl[:], ALU.add)
                v.tensor_scalar(xn, xpl[:], lngc[:, k:k + 1], lnbc[:, k:k + 1],
                                ALU.mult, ALU.add)

        emit_ln_apply(p, xt, TL, {'v': apply_v}, "ln1", scope=sc)

        with nc.Block() as blk:
            @blk.sync
            def _(s):
                for k in range(8):
                    p.dma(s, xplus_o[k * 128:(k + 1) * 128, :], xt[:, k * TL:(k + 1) * TL], q="st")
                p.dma_wait_all(s, q="st")

    wqkb = p.sb("wqkb", [128, 8 * 2048], BF16)
    with p.scope() as sc:
        whb = p.sb("whb", [128, 8 * 4096], BF16, scope=sc)
        wstg = p.sb("wstg", [128, 4096], scope=sc)
        for k in range(8):
            with nc.Block() as blk:
                @blk.sync
                def _(s, k=k):
                    p.dma(s, wstg[:], w_h[k * 128:(k + 1) * 128, :])
                    p.dma_wait_all(s)
            with nc.Block() as blk:
                @blk.vector
                def _(v, k=k):
                    v.tensor_copy(whb[:, k * 4096:(k + 1) * 4096], wstg[:])
            with nc.Block() as blk:
                @blk.sync
                def _(s, k=k):
                    p.dma(s, wstg[:, 0:2048], w_qk[k * 128:(k + 1) * 128, :])
                    p.dma_wait_all(s)
            with nc.Block() as blk:
                @blk.vector
                def _(v, k=k):
                    v.tensor_copy(wqkb[:, k * 2048:(k + 1) * 2048], wstg[:, 0:2048])

        vtile = p.sb("vtile", [128, TL], scope=sc)
        gtile = p.sb("gtile", [128, TL], scope=sc)
        vgt = p.sb("vgt", [128, TL], BF16, scope=sc)
        ps_v = [p.ps([128, 512], scope=sc) for _ in range(4)]
        ps_g = [p.ps([128, 512], scope=sc) for _ in range(4)]

        for m in range(16):
            with nc.Block() as blk:
                @blk.tensor
                def _(t, m=m):
                    for c in range(4):
                        for k in range(8):
                            t.matmul(ps_v[c][:], whb[:, k * 4096 + m * 128:k * 4096 + (m + 1) * 128],
                                     xnb[:, k * TL + c * 512:k * TL + c * 512 + 512],
                                     start=(k == 0), stop=(k == 7))
                    for c in range(4):
                        for k in range(8):
                            t.matmul(ps_g[c][:], whb[:, k * 4096 + (m + 16) * 128:k * 4096 + (m + 17) * 128],
                                     xnb[:, k * TL + c * 512:k * TL + c * 512 + 512],
                                     start=(k == 0), stop=(k == 7))
            with nc.Block() as blk:
                @blk.scalar
                def _(s, m=m):
                    for c in range(4):
                        s.activation(vtile[:, c * 512:(c + 1) * 512], ps_v[c][:], AF.Silu,
                                     bias=bhc[:, m:m + 1])
                        s.activation(gtile[:, c * 512:(c + 1) * 512], ps_g[c][:], AF.Silu,
                                     bias=bhc[:, m + 16:m + 17])
            with nc.Block() as blk:
                @blk.vector
                def _(v, m=m):
                    v.tensor_mul(vgt[:], vtile[:], gtile[:])
            with nc.Block() as blk:
                @blk.sync
                def _(s, m=m):
                    p.dma(s, vg_o[m * 128:(m + 1) * 128, :], vgt[:], q="st")
                    p.dma_wait_all(s, q="st")

    qkfull = p.sb("qkfull", [128, 16 * TL], BF16)
    sred = p.sb("sred", [128, 16 * 2])
    tstage = p.sb("tstage", [128, 2048], BF16)

    with p.scope() as sc:
        ps_q = [p.ps([128, 512], scope=sc) for _ in range(4)]
        for m in range(16):
            with nc.Block() as blk:
                @blk.tensor
                def _(t, m=m):
                    for c in range(4):
                        for k in range(8):
                            t.matmul(ps_q[c][:], wqkb[:, k * 2048 + m * 128:k * 2048 + (m + 1) * 128],
                                     xnb[:, k * TL + c * 512:k * TL + c * 512 + 512],
                                     start=(k == 0), stop=(k == 7))
            with nc.Block() as blk:
                @blk.scalar
                def _(s, m=m):
                    for c in range(4):
                        s.activation(qkfull[:, m * TL + c * 512:m * TL + c * 512 + 512],
                                     ps_q[c][:], AF.Silu, bias=bqkc[:, m:m + 1])
            with nc.Block() as blk:
                @blk.vector
                def _(v, m=m):
                    v.reduce_sum(sred[:, 2 * m:2 * m + 1], qkfull[:, m * TL:m * TL + TB], AX.X)
                    v.reduce_sum(sred[:, 2 * m + 1:2 * m + 2], qkfull[:, m * TL + TB:m * TL + TL], AX.X)

        with nc.Block() as blk:
            @blk.sync
            def _(s):
                for m in range(16):
                    p.dma(s, scol_o[m * 128:(m + 1) * 128, :], sred[:, 2 * m:2 * m + 2], q="st")

    with p.scope() as sc:
        pst = [p.ps([128, 128], BF16, scope=sc) for _ in range(8)]
        for j in range(16):
            for half in range(2):
                with nc.Block() as blk:
                    @blk.tensor
                    def _(t, j=j, half=half):
                        for mi in range(8):
                            m = half * 8 + mi
                            t.transpose(pst[mi][:], qkfull[:, m * TL + j * 128:m * TL + (j + 1) * 128],
                                        ident[:])
                with nc.Block() as blk:
                    @blk.vector
                    def _(v, j=j, half=half):
                        for mi in range(8):
                            m = half * 8 + mi
                            v.tensor_copy(tstage[:, m * 128:(m + 1) * 128], pst[mi][:])
            with nc.Block() as blk:
                @blk.sync
                def _(s, j=j):
                    p.dma(s, qkT_o[j * 128:(j + 1) * 128, :], tstage[:], q="st")
                    p.dma_wait_all(s, q="st")

    with nc.Block() as blk:
        @blk.sync
        def _(s):
            p.dma_wait_all(s, q="st")

    return p.done()



def build_phase2():
    p = Phase("ph2")
    nc = p.nc
    qkT = p.din("qkT_full", [NC * TL, 2048], BF16)
    qkT_my = p.din("qkT_my", [NC * TL, 256], BF16)
    sc_rows = p.din("scol_rows", [NC * 2, 2048])
    sc_cols = p.din("scol_cols", [NC * 128, 4])
    g0c = p.din("g0c", [128, 2])
    b0c = p.din("b0c", [128, 2])
    g1r = p.din("g1r", [1, 2048])
    b1r = p.din("b1r", [1, 2048])
    rs_o = p.dout("rs", [128, 4])
    dbg_o = p.dout("dbg_sim", [128, 4 * 2048])
    dbg2_o = p.dout("dbg_nm", [128, 12])

    ld = p.sem("ld")
    tile_s = p.sem("tile")
    cons_s = p.sem("cons")

    qtile = [p.sb(f"qt{i}", [128, 2048], BF16) for i in range(2)]
    qtmy = [p.sb(f"qm{i}", [128, 256], BF16) for i in range(2)]
    gsb = p.sb("gsb", [128, 4 * 2048])
    srow = [p.sb(f"srow{b}", [1, 2048]) for b in range(2)]
    scolsum = p.sb("scolsum", [128, 4])
    sstage = p.sb("sstage", [128, 4])
    r34 = [p.sb(f"r34_{b}", [1, 2048]) for b in range(2)]
    g0sb = p.sb("g0sb", [128, 2])
    b0sb = p.sb("b0sb", [128, 2])
    g1sb = p.sb("g1sb", [1, 2048])
    b1sb = p.sb("b1sb", [1, 2048])
    ones1 = p.sb("ones1p2", [1, 128])
    c2c = p.sb("c2c", [128, 4])
    rsred = p.sb("rsred", [128, 8])
    negmax = p.sb("negmax", [128, 4])
    zrec = p.sb("zrec", [128, 4])
    attn = p.sb("attn", [128, 2048])


    with nc.Block() as blk:
        @blk.sync
        def _(s):
            p.dma(s, srow[0][0:1, :], sc_rows[0:1, :])
            p.dma(s, srow[1][0:1, :], sc_rows[1:2, :])
            p.dma(s, scolsum[:], sc_cols[0:128, :])
            p.dma(s, g0sb[:], g0c)
            p.dma(s, b0sb[:], b0c)
            p.dma(s, g1sb[:], g1r)
            p.dma(s, b1sb[:], b1r)
            p.dma_wait_all(s)
    stg_r = [p.sb(f"stg_r{b}", [1, 2048]) for b in range(2)]
    for i in range(1, NC):
        with nc.Block() as blk:
            @blk.sync
            def _(s, i=i):
                p.dma(s, stg_r[0][0:1, :], sc_rows[i * 2:i * 2 + 1, :])
                p.dma(s, stg_r[1][0:1, :], sc_rows[i * 2 + 1:i * 2 + 2, :])
                p.dma(s, sstage[:], sc_cols[i * 128:(i + 1) * 128, :])
                p.dma_wait_all(s)
        with nc.Block() as blk:
            @blk.vector
            def _(v, i=i):
                v.tensor_add(srow[0][0:1, :], srow[0][0:1, :], stg_r[0][0:1, :])
                v.tensor_add(srow[1][0:1, :], srow[1][0:1, :], stg_r[1][0:1, :])
                v.tensor_add(scolsum[:], scolsum[:], sstage[:])

    with nc.Block() as blk:
        @blk.gpsimd
        def _(g):
            g.memset(ones1[:], 1.0)

        @blk.vector
        def _(v):
            for b in range(2):
                v.tensor_mul(r34[b][0:1, :], g1sb[0:1, :], srow[b][0:1, :])
                v.scalar_tensor_tensor(r34[b][0:1, :], b1sb[0:1, :], float(TG), r34[b][0:1, :],
                                       ALU.mult, ALU.add)
                v.tensor_single_scalar(r34[b][0:1, :], r34[b][0:1, :], 1.0 / TG, ALU.mult)
            for dt_ in range(2):
                for b in range(2):
                    v.tensor_scalar(c2c[:, 2 * dt_ + b:2 * dt_ + b + 1],
                                    scolsum[:, 2 * dt_ + b:2 * dt_ + b + 1],
                                    g0sb[:, dt_:dt_ + 1], 1.0 / TG, ALU.mult, ALU.mult)

    sc_g = p.scope()
    psG = [p.ps([128, 512], scope=sc_g) for _ in range(8)]
    with nc.Block() as blk:
        @blk.sync
        def _(s):
            n = 0
            for b in range(2):
                for kc in range(64):
                    cc, i = divmod(kc, 8)
                    row = cc * TL + b * TB + i * 128
                    if n >= 2:
                        s.wait_ge(tile_s[0], n - 1)
                    p.dma(s, qtile[n % 2][:], qkT[row:row + 128, :], q="qld")
                    p.dma(s, qtmy[n % 2][:], qkT_my[row:row + 128, :], q="qld")
                    n += 1

        @blk.tensor
        def _(t):
            qs = p.sem("qld")
            n = 0
            for b in range(2):
                for kc in range(64):
                    t.wait_ge(qs[0], 32 * (n + 1))
                    src_t = qtile[n % 2]
                    src_m = qtmy[n % 2]
                    for mt in range(2):
                        for ch in range(4):
                            inst = t.matmul(psG[mt * 4 + ch][:],
                                            src_m[:, mt * 128:(mt + 1) * 128],
                                            src_t[:, ch * 512:(ch + 1) * 512],
                                            start=(kc == 0), stop=(kc == 63))
                    inst.then_inc(tile_s[0], 1)
                    n += 1
                if b == 0:
                    t.wait_ge(cons_s[0], 1)

        @blk.vector
        def _(v):
            for b in range(2):
                v.wait_ge(tile_s[0], 64 * (b + 1))
                for mt in range(2):
                    for ch in range(4):
                        inst = v.tensor_copy(gsb[:, (b * 2 + mt) * 2048 + ch * 512:(b * 2 + mt) * 2048 + ch * 512 + 512],
                                             psG[mt * 4 + ch][:])
                inst.then_inc(cons_s[0], 1)

    sc_g.close()

    psR1 = p.ps([128, 512])
    psR2 = p.ps([128, 512])
    psR34 = p.ps([128, 512])
    for b in range(2):
        for mt in range(2):
            g_i = (b * 2 + mt)
            for ch in range(4):
                with nc.Block() as blk:
                    @blk.tensor
                    def _(t, b=b, ch=ch):
                        t.matmul(psR1[:], ones1[:], g1sb[0:1, ch * 512:(ch + 1) * 512], start=True, stop=True)
                        t.matmul(psR2[:], ones1[:], b1sb[0:1, ch * 512:(ch + 1) * 512], start=True, stop=True)
                        t.matmul(psR34[:], ones1[:], r34[b][0:1, ch * 512:(ch + 1) * 512], start=True, stop=True)
                with nc.Block() as blk:
                    @blk.vector
                    def _(v, b=b, mt=mt, g_i=g_i, ch=ch):
                        sim = gsb[:, g_i * 2048 + ch * 512:g_i * 2048 + ch * 512 + 512]
                        v.tensor_scalar(sim, sim, g0sb[:, mt:mt + 1], 1.0 / TG, ALU.mult, ALU.mult)
                        v.tensor_tensor(sim, sim, psR1[:], ALU.mult)
                        v.scalar_tensor_tensor(attn[:, 0:512], psR2[:], c2c[:, 2 * mt + b:2 * mt + b + 1],
                                               sim, ALU.mult, ALU.add)
                        v.tensor_copy(sim, attn[:, 0:512])
                        v.scalar_tensor_tensor(attn[:, 0:512], psR34[:], b0sb[:, mt:mt + 1],
                                               sim, ALU.mult, ALU.add)
                        v.tensor_copy(sim, attn[:, 0:512])
    with nc.Block() as blk:
        @blk.sync
        def _(s):
            p.dma(s, dbg_o[:, :], gsb[:, :], q="st")
            p.dma_wait_all(s, q="st")

    for g_i in range(4):
        with nc.Block() as blk:
            @blk.vector
            def _(v, g_i=g_i):
                sim = gsb[:, g_i * 2048:(g_i + 1) * 2048]
                v.reduce_max(negmax[:, g_i:g_i + 1], sim, AX.X, negate=True)
        with nc.Block() as blk:
            @blk.scalar
            def _(s, g_i=g_i):
                sim = gsb[:, g_i * 2048:(g_i + 1) * 2048]
                s.activation(sim, sim, AF.Exp, bias=negmax[:, g_i:g_i + 1],
                             accum_out=zrec[:, g_i:g_i + 1])
        with nc.Block() as blk:
            @blk.vector
            def _(v, g_i=g_i):
                v.reciprocal(zrec[:, g_i:g_i + 1], zrec[:, g_i:g_i + 1])
        with nc.Block() as blk:
            @blk.vector
            def _(v, g_i=g_i):
                sim = gsb[:, g_i * 2048:(g_i + 1) * 2048]
                v.tensor_scalar(attn[:], sim, zrec[:, g_i:g_i + 1], None, ALU.mult)
                v.reduce_sum(rsred[:, g_i:g_i + 1], attn[:], AX.X)

    with nc.Block() as blk:
        @blk.sync
        def _(s):
            p.dma(s, rs_o[:, :], rsred[:, 0:4], q="st")
            p.dma(s, dbg2_o[:, 0:4], negmax[:, 0:4], q="st")
            p.dma(s, dbg2_o[:, 4:8], zrec[:, 0:4], q="st")
            p.dma_wait_all(s, q="st")

    return p.done()


def host_phase2_inputs(inputs, r1s):
    qkT_full = np.concatenate([r["qkT"] for r in r1s], axis=0)
    sc = np.stack([r["scol"] for r in r1s], 0)
    sc_rows = np.ascontiguousarray(sc.transpose(0, 2, 1).reshape(NC * 2, 2048))
    osg = np.asarray(inputs["L0_attn_a_os_gamma"], np.float32)
    osb = np.asarray(inputs["L0_attn_a_os_beta"], np.float32)
    maps = []
    for c in range(NC):
        lo, hi = c * 256, (c + 1) * 256
        sc_cols = np.zeros((NC * 128, 4), np.float32)
        for i in range(NC):
            seg = sc[i, lo:hi, :]
            sc_cols[i * 128:(i + 1) * 128, 0:2] = seg[:128]
            sc_cols[i * 128:(i + 1) * 128, 2:4] = seg[128:]
        g0c = np.stack([osg[0, lo:lo + 128], osg[0, lo + 128:hi]], 1)
        b0c = np.stack([osb[0, lo:lo + 128], osb[0, lo + 128:hi]], 1)
        maps.append({
            "qkT_full": qkT_full,
            "qkT_my": np.ascontiguousarray(qkT_full[:, lo:hi]),
            "scol_rows": sc_rows,
            "scol_cols": np.ascontiguousarray(sc_cols),
            "g0c": np.ascontiguousarray(g0c),
            "b0c": np.ascontiguousarray(b0c),
            "g1r": np.ascontiguousarray(osg[1:2, :]),
            "b1r": np.ascontiguousarray(osb[1:2, :]),
        })
    return maps


def host_phase1_inputs(inputs, xs):
    wh = f32a(inputs["L0_attn_a_W_hidden"])
    wqk = f32a(inputs["L0_attn_a_W_qk"])
    common = {
        "W_hidden": np.ascontiguousarray(wh),
        "b_hidden": vcols(inputs["L0_attn_a_b_hidden"]),
        "W_qk": np.ascontiguousarray(wqk),
        "b_qk": vcols(inputs["L0_attn_a_b_qk"]),
        "lng": vcols(inputs["L0_attn_a_ln_g"]),
        "lnb": vcols(inputs["L0_attn_a_ln_b"]),
        "ident_i": IDENT128,
    }
    return [dict(common, x_fm=xs[c]) for c in range(NC)]


def f32a(x):
    return np.asarray(x, np.float32)




def emit_ffn_expert(p, xnf, res_dram, out_dram, w1c, w3c, w2c):
    nc = p.nc
    with p.scope() as sc:
        w1b = p.sb("ffn_w1b", [128, 8 * 2816], BF16, scope=sc)
        w3b = p.sb("ffn_w3b", [128, 8 * 2816], BF16, scope=sc)
        w2b = p.sb("ffn_w2b", [128, 22 * 1024], BF16, scope=sc)
        wstg = p.sb("ffn_wstg", [128, 2816], scope=sc)
        hh = p.sb("ffn_hh", [128, 22 * 512], BF16, scope=sc)
        gt = p.sb("ffn_gt", [128, 512], scope=sc)
        xres = p.sb("ffn_xres", [128, 512], scope=sc)
        psA = p.ps([128, 512], scope=sc)
        psB = p.ps([128, 512], scope=sc)
        psC = [p.ps([128, 512], scope=sc) for _ in range(2)]

        for k in range(8):
            for (wsrc, wdst, wid) in ((w1c, w1b, 2816), (w3c, w3b, 2816)):
                with nc.Block() as blk:
                    @blk.sync
                    def _(s, k=k, wsrc=wsrc):
                        p.dma(s, wstg[:], wsrc[k * 128:(k + 1) * 128, :])
                        p.dma_wait_all(s)
                with nc.Block() as blk:
                    @blk.vector
                    def _(v, k=k, wdst=wdst):
                        v.tensor_copy(wdst[:, k * 2816:(k + 1) * 2816], wstg[:])
        for k in range(22):
            with nc.Block() as blk:
                @blk.sync
                def _(s, k=k):
                    p.dma(s, wstg[:, 0:1024], w2c[k * 128:(k + 1) * 128, :])
                    p.dma_wait_all(s)
            with nc.Block() as blk:
                @blk.vector
                def _(v, k=k):
                    v.tensor_copy(w2b[:, k * 1024:(k + 1) * 1024], wstg[:, 0:1024])

        for ch in range(4):
            for mf in range(22):
                with nc.Block() as blk:
                    @blk.tensor
                    def _(t, ch=ch, mf=mf):
                        for k in range(8):
                            t.matmul(psA[:], w1b[:, k * 2816 + mf * 128:k * 2816 + (mf + 1) * 128],
                                     xnf[:, k * TL + ch * 512:k * TL + ch * 512 + 512],
                                     start=(k == 0), stop=(k == 7))
                        for k in range(8):
                            t.matmul(psB[:], w3b[:, k * 2816 + mf * 128:k * 2816 + (mf + 1) * 128],
                                     xnf[:, k * TL + ch * 512:k * TL + ch * 512 + 512],
                                     start=(k == 0), stop=(k == 7))
                with nc.Block() as blk:
                    @blk.scalar
                    def _(s, ch=ch, mf=mf):
                        s.activation(gt[:], psA[:], AF.Silu)
                with nc.Block() as blk:
                    @blk.vector
                    def _(v, ch=ch, mf=mf):
                        v.tensor_tensor(hh[:, mf * 512:(mf + 1) * 512], gt[:], psB[:], ALU.mult)
            for mo in range(8):
                with nc.Block() as blk:
                    @blk.tensor
                    def _(t, ch=ch, mo=mo):
                        for k in range(22):
                            t.matmul(psC[mo % 2][:], w2b[:, k * 1024 + mo * 128:k * 1024 + (mo + 1) * 128],
                                     hh[:, k * 512:(k + 1) * 512],
                                     start=(k == 0), stop=(k == 21))

                    @blk.sync
                    def _(s, ch=ch, mo=mo):
                        p.dma(s, xres[:, 0:512],
                              res_dram[mo * 128:(mo + 1) * 128, ch * 512:(ch + 1) * 512], q="ffnxl")
                        p.dma_wait_all(s, q="ffnxl")
                with nc.Block() as blk:
                    @blk.vector
                    def _(v, ch=ch, mo=mo):
                        v.scalar_tensor_tensor(xres[:, 0:512], psC[mo % 2][:], 1.0, xres[:, 0:512],
                                               ALU.mult, ALU.add)
                with nc.Block() as blk:
                    @blk.sync
                    def _(s, ch=ch, mo=mo):
                        p.dma(s, out_dram[mo * 128:(mo + 1) * 128, ch * 512:(ch + 1) * 512],
                              xres[:, 0:512], q="st")
                        p.dma_wait_all(s, q="st")



def build_phase3():
    p = Phase("ph3")
    nc = p.nc
    xplus = p.din("xplus_fm", [D, TL])
    vg = p.din("vg", [2048, TL], BF16)
    rsc = p.din("rs_cols", [128, 16 * 2])
    w_out = p.din("W_out", [2048, D])
    b_out = p.din("b_out", [128, 8])
    w1c = p.din("w1c", [D, 2816])
    w3c = p.din("w3c", [D, 2816])
    w2c = p.din("w2c", [2816, D])
    x2_tmp = p.dout("x2_fm", [D, TL])

    x3_o = p.dout("x3_fm", [D, TL])

    rssb = p.sb("rssb", [128, 32])
    bosb = p.sb("bosb", [128, 8])
    xres = p.sb("xres", [128, 512])
    xnf = p.sb("xnf", [128, 8 * TL], BF16)

    sc_x2 = p.scope()
    x2 = p.sb("x2", [128, 8 * TL], scope=sc_x2)
    with p.scope() as sc:
        vgb = p.sb("vgb", [128, 16 * TL], BF16, scope=sc)
        wob = p.sb("wob", [128, 16 * 1024], BF16, scope=sc)
        wstg = p.sb("wstg3", [128, 1024], scope=sc)
        ps_o = [p.ps([128, 512], scope=sc) for _ in range(4)]

        with nc.Block() as blk:
            @blk.sync
            def _(s):
                for k in range(16):
                    p.dma(s, vgb[:, k * TL:(k + 1) * TL], vg[k * 128:(k + 1) * 128, :])
                p.dma(s, rssb[:], rsc)
                p.dma(s, bosb[:], b_out)
                p.dma_wait_all(s)

        with nc.Block() as blk:
            @blk.vector
            def _(v):
                for k in range(16):
                    for b in range(2):
                        sl = vgb[:, k * TL + b * TB:k * TL + (b + 1) * TB]
                        v.tensor_scalar(sl, sl, rssb[:, 2 * k + b:2 * k + b + 1], None, ALU.mult)

        for k in range(16):
            with nc.Block() as blk:
                @blk.sync
                def _(s, k=k):
                    p.dma(s, wstg[:], w_out[k * 128:(k + 1) * 128, :])
                    p.dma_wait_all(s)
            with nc.Block() as blk:
                @blk.vector
                def _(v, k=k):
                    v.tensor_copy(wob[:, k * 1024:(k + 1) * 1024], wstg[:])

        for mt in range(8):
            with nc.Block() as blk:
                @blk.tensor
                def _(t, mt=mt):
                    for ch in range(4):
                        for k in range(16):
                            t.matmul(ps_o[ch][:], wob[:, k * 1024 + mt * 128:k * 1024 + (mt + 1) * 128],
                                     vgb[:, k * TL + ch * 512:k * TL + ch * 512 + 512],
                                     start=(k == 0), stop=(k == 15))
            with nc.Block() as blk:
                @blk.sync
                def _(s, mt=mt):
                    for ch in range(4):
                        p.dma(s, xres[:, 0:512] if False else x2[:, mt * TL + ch * 512:mt * TL + ch * 512 + 512],
                              xplus[mt * 128:(mt + 1) * 128, ch * 512:(ch + 1) * 512], q="xl")
                    p.dma_wait_all(s, q="xl")
            with nc.Block() as blk:
                @blk.scalar
                def _(s, mt=mt):
                    pass

                @blk.vector
                def _(v, mt=mt):
                    for ch in range(4):
                        sl = x2[:, mt * TL + ch * 512:mt * TL + ch * 512 + 512]
                        v.scalar_tensor_tensor(sl, ps_o[ch][:], 1.0, sl, ALU.mult, ALU.add)
                        v.tensor_scalar(sl, sl, bosb[:, mt:mt + 1], None, ALU.add)

    with nc.Block() as blk:
        @blk.sync
        def _(s):
            for k in range(8):
                p.dma(s, x2_tmp[k * 128:(k + 1) * 128, :], x2[:, k * TL:(k + 1) * TL], q="st")
            p.dma_wait_all(s, q="st")

    with p.scope() as sc:
        def apply_v(v, c, psMu, psRs):
            for k in range(8):
                xs = x2[:, k * TL + c * 512:k * TL + c * 512 + 512]
                xn = xnf[:, k * TL + c * 512:k * TL + c * 512 + 512]
                v.tensor_tensor(xres[:, 0:512], xs, psMu[:], ALU.subtract)
                v.tensor_tensor(xn, xres[:, 0:512], psRs[:], ALU.mult)

        emit_ln_apply(p, x2, TL, {'v': apply_v}, "ln3", scope=sc)
    sc_x2.close()

    emit_ffn_expert(p, xnf, x2_tmp, x3_o, w1c, w3c, w2c)

    with nc.Block() as blk:
        @blk.sync
        def _(s):
            p.dma_wait_all(s, q="st")

    return p.done()


def host_phase3_inputs(inputs, r1s, r2s):
    rs_all = np.zeros((2048, 2), np.float32)
    for c in range(NC):
        rr = r2s[c]["rs"]
        for mt in range(2):
            for b in range(2):
                rs_all[c * 256 + mt * 128:c * 256 + (mt + 1) * 128, b] = rr[:, b * 2 + mt]
    rs_cols = np.zeros((128, 32), np.float32)
    for k in range(16):
        for b in range(2):
            rs_cols[:, 2 * k + b] = rs_all[k * 128:(k + 1) * 128, b]
    w1 = np.asarray(inputs["L0_ffn_w1"], np.float32)
    w3 = np.asarray(inputs["L0_ffn_w3"], np.float32)
    w2 = np.asarray(inputs["L0_ffn_w2"], np.float32)
    maps = []
    for c in range(NC):
        maps.append({
            "xplus_fm": None,
            "vg": r1s[c]["vg"],
            "rs_cols": np.ascontiguousarray(rs_cols),
            "W_out": np.ascontiguousarray(np.asarray(inputs["L0_attn_a_W_out"], np.float32)),
            "b_out": vcols(inputs["L0_attn_a_b_out"]),
            "w1c": np.ascontiguousarray(w1[c]),
            "w3c": np.ascontiguousarray(w3[c]),
            "w2c": np.ascontiguousarray(w2[c]),
        })
    return maps



TLH = 1152
THP = 8 * 512 - TLH * 2
TLN = 2560


def build_phase4a():
    p = Phase("p4a")
    nc = p.nc
    x3 = p.din("x3_fm", [D, TL])
    tails = p.din("tails", [D, 256])
    wqkv = p.din("wqkv", [D, 3072])
    w_ob = p.din("w_outb", [D, D])
    gcol = p.din("gcol", [128, 8])
    bcol = p.din("bcol", [128, 8])
    maskw0 = p.din("maskw0", [128, 256])
    maskrest = p.din("maskrest", [128, 256])
    ident_in = p.din("ident_i", [128, 128], BF16)

    x4_o = p.dout("x4_fm", [D, TL])

    xnbb = p.sb("xnbb", [128, 8 * TLN], BF16)
    wqb = p.sb("wqb", [128, 8 * 3072], BF16)
    gsb = p.sb("gsb4", [128, 8])
    bsb = p.sb("bsb4", [128, 8])
    mk0 = p.sb("mk0", [128, 256])
    mkr = p.sb("mkr", [128, 256])
    ident = p.sb("ident4", [128, 128], BF16)

    with p.scope() as sc:
        xt = p.sb("xt4", [128, 8 * TL], scope=sc)
        tailt = p.sb("tailt", [128, 8 * 512], scope=sc)
        xscr = p.sb("xscr4", [128, 512], scope=sc)

        with nc.Block() as blk:
            @blk.gpsimd
            def _(g):
                g.memset(tailt[:], 1.0)
                g.memset(xnbb[:], 0.0)

            @blk.sync
            def _(s):
                for k in range(8):
                    p.dma(s, xt[:, k * TL:(k + 1) * TL], x3[k * 128:(k + 1) * 128, :])
                    p.dma(s, tailt[:, k * 512:k * 512 + 256], tails[k * 128:(k + 1) * 128, :])
                p.dma(s, gsb[:], gcol)
                p.dma(s, bsb[:], bcol)
                p.dma(s, mk0[:], maskw0)
                p.dma(s, mkr[:], maskrest)
                p.dma(s, ident[:], ident_in)
                p.dma_wait_all(s)

        def apply_own(v, c, psMu, psRs):
            b = (c * 512) // TB
            wo = (c * 512) % TB
            for k in range(8):
                xs = xt[:, k * TL + c * 512:k * TL + c * 512 + 512]
                xn = xnbb[:, k * TLN + b * TLH + 128 + wo:k * TLN + b * TLH + 128 + wo + 512]
                v.tensor_tensor(xscr[:], xs, psMu[:], ALU.subtract)
                v.tensor_tensor(xscr[:], xscr[:], psRs[:], ALU.mult)
                v.tensor_scalar(xn, xscr[:], gsb[:, k:k + 1], bsb[:, k:k + 1],
                                ALU.mult, ALU.add)

        with p.scope() as lnsc:
            emit_ln_apply(p, xt, TL, {'v': apply_own}, "ln4o", scope=lnsc)

        def apply_halo(v, c, psMu, psRs):
            for k in range(8):
                xs = tailt[:, k * 512:(k + 1) * 512]
                v.tensor_tensor(xscr[:], xs, psMu[:], ALU.subtract)
                v.tensor_tensor(xscr[:], xscr[:], psRs[:], ALU.mult)
                v.tensor_scalar(xscr[:], xscr[:], gsb[:, k:k + 1], bsb[:, k:k + 1],
                                ALU.mult, ALU.add)
                v.tensor_copy(xnbb[:, k * TLN + 0:k * TLN + 128], xscr[:, 0:128])
                v.tensor_copy(xnbb[:, k * TLN + TLH:k * TLN + TLH + 128], xscr[:, 128:256])

        with p.scope() as lnsc:
            emit_ln_apply_sized(p, tailt, 512, {'v': apply_halo}, "ln4h", scope=lnsc)

    with p.scope() as sc:
        wstg = p.sb("wstg4", [128, 3072], scope=sc)
        for k in range(8):
            with nc.Block() as blk:
                @blk.sync
                def _(s, k=k):
                    p.dma(s, wstg[:], wqkv[k * 128:(k + 1) * 128, :])
                    p.dma_wait_all(s)
            with nc.Block() as blk:
                @blk.vector
                def _(v, k=k):
                    v.tensor_copy(wqb[:, k * 3072:(k + 1) * 3072], wstg[:])

    aout = p.sb("aout", [128, 8 * TL], BF16)
    with p.scope() as sc:
        qf = p.sb("qf", [128, 8 * TLH], BF16, scope=sc)
        kf = p.sb("kf", [128, 8 * TLH], BF16, scope=sc)
        vf = p.sb("vf", [128, 8 * TLH], BF16, scope=sc)
        vtm = p.sb("vtm", [128, 16 * 9 * 64], BF16, scope=sc)
        smf = p.sb("smf", [128, 4 * 256], scope=sc)
        pbf = p.sb("pbf", [128, 4 * 256], BF16, scope=sc)
        ptb = p.sb("ptb", [128, 4 * 256], BF16, scope=sc)
        nmx = p.sb("nmx", [128, 16], scope=sc)
        zzr = p.sb("zzr", [128, 16], scope=sc)
        psq = [p.ps([128, 512], scope=sc) for _ in range(4)]
        pstbt = [p.ps([128, 512], BF16, scope=sc, name=f"ptr{i}") for i in range(2)]

        def pstb(i):
            return pstbt[i // 4][:, (i % 4) * 128:(i % 4) * 128 + 128]

        psav = [p.ps([64, 512], scope=sc, name=f"pav{i}") for i in range(2)]

        for b in range(2):
            boff = b * TLH
            chs = [(0, 512), (512, 512), (1024, 128)]
            for mt in range(24):
                with nc.Block() as blk:
                    @blk.tensor
                    def _(t, mt=mt, boff=boff):
                        for ci, (c0, cl) in enumerate(chs):
                            for k in range(8):
                                t.matmul(psq[ci][:, 0:cl],
                                         wqb[:, k * 3072 + mt * 128:k * 3072 + (mt + 1) * 128],
                                         xnbb[:, k * TLN + boff + c0:k * TLN + boff + c0 + cl],
                                         start=(k == 0), stop=(k == 7))
                with nc.Block() as blk:
                    @blk.vector
                    def _(v, mt=mt):
                        dst = [qf, kf, vf][mt // 8]
                        m = mt % 8
                        for ci, (c0, cl) in enumerate(chs):
                            v.tensor_copy(dst[:, m * TLH + c0:m * TLH + c0 + cl], psq[ci][:, 0:cl])

            for half in range(2):
                pass
            for h in range(16):
                with nc.Block() as blk:
                    @blk.tensor
                    def _(t, h=h):
                        for j in range(8):
                            hr = (h % 2) * 64
                            t.transpose(pstb(j)[:, 0:64],
                                        vf[hr:hr + 64,
                                           (h // 2) * TLH + j * 128:(h // 2) * TLH + j * 128 + 128],
                                        ident[hr:hr + 64, hr:hr + 64])
                with nc.Block() as blk:
                    @blk.vector
                    def _(v, h=h):
                        for j in range(8):
                            v.tensor_copy(vtm[:, (h * 9 + j) * 64:(h * 9 + j) * 64 + 64], pstb(j)[:, 0:64])
                with nc.Block() as blk:
                    @blk.tensor
                    def _(t, h=h):
                        hr = (h % 2) * 64
                        t.transpose(pstb(0)[:, 0:64],
                                    vf[hr:hr + 64,
                                       (h // 2) * TLH + 8 * 128:(h // 2) * TLH + 9 * 128],
                                    ident[hr:hr + 64, hr:hr + 64])
                with nc.Block() as blk:
                    @blk.vector
                    def _(v, h=h):
                        v.tensor_copy(vtm[:, (h * 9 + 8) * 64:(h * 9 + 8) * 64 + 64], pstb(0)[:, 0:64])

            for h in range(16):
                hrow = (h % 2) * 64
                hmt = h // 2
                for half in range(2):
                    wl = [half * 4 + i for i in range(4)]
                    with nc.Block() as blk:
                        @blk.tensor
                        def _(t, h=h, hrow=hrow, hmt=hmt, wl=wl):
                            for i, w in enumerate(wl):
                                t.matmul(psq[i][:, 0:256],
                                         qf[hrow:hrow + 64, hmt * TLH + 128 + w * 128:hmt * TLH + 128 + (w + 1) * 128],
                                         kf[hrow:hrow + 64, hmt * TLH + w * 128:hmt * TLH + w * 128 + 256],
                                         start=True, stop=True)
                    with nc.Block() as blk:
                        @blk.vector
                        def _(v, h=h, wl=wl, half=half):
                            for i, w in enumerate(wl):
                                mask = mk0 if w == 0 else mkr
                                v.scalar_tensor_tensor(smf[:, i * 256:(i + 1) * 256],
                                                       psq[i][:, 0:256], 0.125, mask[:], ALU.mult, ALU.add)
                            for i in range(4):
                                v.reduce_max(nmx[:, half * 4 + i:half * 4 + i + 1],
                                             smf[:, i * 256:(i + 1) * 256], AX.X, negate=True)
                    with nc.Block() as blk:
                        @blk.scalar
                        def _(s, h=h, half=half):
                            for i in range(4):
                                s.activation(pbf[:, i * 256:(i + 1) * 256], smf[:, i * 256:(i + 1) * 256],
                                             AF.Exp, bias=nmx[:, half * 4 + i:half * 4 + i + 1],
                                             accum_out=zzr[:, half * 4 + i:half * 4 + i + 1])
                    with nc.Block() as blk:
                        @blk.vector
                        def _(v, half=half):
                            v.reciprocal(zzr[:, half * 4:half * 4 + 4], zzr[:, half * 4:half * 4 + 4])
                    with nc.Block() as blk:
                        @blk.vector
                        def _(v, half=half):
                            for i in range(4):
                                v.tensor_scalar(pbf[:, i * 256:(i + 1) * 256], pbf[:, i * 256:(i + 1) * 256],
                                                zzr[:, half * 4 + i:half * 4 + i + 1], None, ALU.mult)
                    with nc.Block() as blk:
                        @blk.tensor
                        def _(t, half=half):
                            for i in range(4):
                                t.transpose(pstb(2 * i), pbf[:, i * 256:i * 256 + 128], ident[:])
                                t.transpose(pstb(2 * i + 1), pbf[:, i * 256 + 128:i * 256 + 256], ident[:])
                    with nc.Block() as blk:
                        @blk.vector
                        def _(v):
                            for i in range(4):
                                v.tensor_copy(ptb[:, i * 256:i * 256 + 128], pstb(2 * i))
                                v.tensor_copy(ptb[:, i * 256 + 128:i * 256 + 256], pstb(2 * i + 1))
                    with nc.Block() as blk:
                        @blk.tensor
                        def _(t, h=h, wl=wl):
                            for i, w in enumerate(wl):
                                for jh in range(2):
                                    t.matmul(psav[i % 2][0:64, (i // 2) * 128:(i // 2) * 128 + 128],
                                             vtm[:, (h * 9 + w + jh) * 64:(h * 9 + w + jh) * 64 + 64],
                                             ptb[:, i * 256 + jh * 128:i * 256 + (jh + 1) * 128],
                                             start=(jh == 0), stop=(jh == 1))
                    with nc.Block() as blk:
                        @blk.vector
                        def _(v, h=h, hrow=hrow, hmt=hmt, b=b, wl=wl):
                            for i, w in enumerate(wl):
                                v.tensor_copy(aout[hrow:hrow + 64,
                                                   hmt * TL + b * TB + w * 128:hmt * TL + b * TB + (w + 1) * 128],
                                              psav[i % 2][0:64, (i // 2) * 128:(i // 2) * 128 + 128])

    with p.scope() as sc2:
        wob = p.sb("wob4", [128, 8 * 1024], BF16, scope=sc2)
        wstg2 = p.sb("wstg4b", [128, 1024], scope=sc2)
        xres = p.sb("xres4", [128, 512], scope=sc2)
        pso = [p.ps([128, 512], scope=sc2) for _ in range(2)]
        for k in range(8):
            with nc.Block() as blk:
                @blk.sync
                def _(s, k=k):
                    p.dma(s, wstg2[:], w_ob[k * 128:(k + 1) * 128, :])
                    p.dma_wait_all(s)
            with nc.Block() as blk:
                @blk.vector
                def _(v, k=k):
                    v.tensor_copy(wob[:, k * 1024:(k + 1) * 1024], wstg2[:])
        for mt in range(8):
            for ch in range(4):
                with nc.Block() as blk:
                    @blk.tensor
                    def _(t, mt=mt, ch=ch):
                        for k in range(8):
                            t.matmul(pso[ch % 2][:], wob[:, k * 1024 + mt * 128:k * 1024 + (mt + 1) * 128],
                                     aout[:, k * TL + ch * 512:k * TL + ch * 512 + 512],
                                     start=(k == 0), stop=(k == 7))

                    @blk.sync
                    def _(s, mt=mt, ch=ch):
                        p.dma(s, xres[:, 0:512],
                              x3[mt * 128:(mt + 1) * 128, ch * 512:(ch + 1) * 512], q="xl")
                        p.dma_wait_all(s, q="xl")
                with nc.Block() as blk:
                    @blk.vector
                    def _(v, mt=mt, ch=ch):
                        v.scalar_tensor_tensor(xres[:, 0:512], pso[ch % 2][:], 1.0, xres[:, 0:512],
                                               ALU.mult, ALU.add)
                with nc.Block() as blk:
                    @blk.sync
                    def _(s, mt=mt, ch=ch):
                        p.dma(s, x4_o[mt * 128:(mt + 1) * 128, ch * 512:(ch + 1) * 512],
                              xres[:, 0:512], q="st")
                        p.dma_wait_all(s, q="st")

    with nc.Block() as blk:
        @blk.sync
        def _(s):
            p.dma_wait_all(s, q="st")

    return p.done()


def host_phase4a_inputs(inputs, x3s):
    wqkv = np.ascontiguousarray(np.asarray(inputs["L0_attn_b_w_qkv"], np.float32))
    wob = np.ascontiguousarray(np.asarray(inputs["L0_attn_b_w_out"], np.float32))
    g = vcols(inputs["L0_attn_b_ln_gamma"])
    bcl = vcols(inputs["L0_attn_b_ln_beta"])
    NEG = np.float32(-1e30)
    i_ = np.arange(128)[:, None]
    j_ = np.arange(256)[None, :]
    mrest = np.where((j_ < 128) & (j_ >= i_) | (j_ >= 128) & (j_ - 128 <= i_), 0.0, NEG).astype(np.float32)
    m0 = mrest.copy()
    m0[:, :128] = NEG
    maps = []
    for c in range(NC):
        if c == 0:
            tails = np.zeros((D, 256), np.float32)
        else:
            prev = x3s[c - 1]
            tails = np.concatenate([prev[:, TB - 128:TB], prev[:, TL - 128:TL]], axis=1)
        maps.append({
            "x3_fm": x3s[c],
            "tails": np.ascontiguousarray(tails),
            "wqkv": wqkv,
            "w_outb": wob,
            "gcol": g, "bcol": bcl,
            "maskw0": (m0 if c == 0 else mrest), "maskrest": mrest,
            "ident_i": IDENT128,
        })
    return maps





def build_phase4b():
    p = Phase("p4b")
    nc = p.nc
    x4 = p.din("x4_fm", [D, TL])
    w1c = p.din("w1c", [D, 2816])
    w3c = p.din("w3c", [D, 2816])
    w2c = p.din("w2c", [2816, D])
    l1w = p.din("l1w", [128, 8])
    l1b = p.din("l1b", [128, 8])

    x5_o = p.dout("x5_fm", [D, TL])
    xn5_o = p.dout("xn5a_fm", [D, TL])
    xn5b_o = p.dout("xn5a_bf", [D, TL], BF16)

    xnf = p.sb("xnf4b", [128, 8 * TL], BF16)

    with p.scope() as sc:
        xt = p.sb("xt4b", [128, 8 * TL], scope=sc)
        xscr = p.sb("xscr4b", [128, 512], scope=sc)
        with nc.Block() as blk:
            @blk.sync
            def _(s):
                for k in range(8):
                    p.dma(s, xt[:, k * TL:(k + 1) * TL], x4[k * 128:(k + 1) * 128, :])
                p.dma_wait_all(s)

        def apply_v(v, c, psMu, psRs):
            for k in range(8):
                xs = xt[:, k * TL + c * 512:k * TL + c * 512 + 512]
                xn = xnf[:, k * TL + c * 512:k * TL + c * 512 + 512]
                v.tensor_tensor(xscr[:], xs, psMu[:], ALU.subtract)
                v.tensor_tensor(xn, xscr[:], psRs[:], ALU.mult)

        with p.scope() as lnsc:
            emit_ln_apply(p, xt, TL, {'v': apply_v}, "ln4b", scope=lnsc)

    emit_ffn_expert(p, xnf, x4, x5_o, w1c, w3c, w2c)

    with p.scope() as sc:
        xt = p.sb("xt4b2", [128, 8 * TL], scope=sc)
        xscr = p.sb("xscr4b2", [128, 512], scope=sc)
        xbf = p.sb("xbf4b", [128, 512], BF16, scope=sc)
        wsb = p.sb("l1wsb", [128, 8], scope=sc)
        bsb = p.sb("l1bsb", [128, 8], scope=sc)
        with nc.Block() as blk:
            @blk.sync
            def _(s):
                p.dma_wait_all(s, q="st")
        with nc.Block() as blk:
            @blk.sync
            def _(s):
                for k in range(8):
                    p.dma(s, xt[:, k * TL:(k + 1) * TL], x5_o[k * 128:(k + 1) * 128, :])
                p.dma(s, wsb[:], l1w)
                p.dma(s, bsb[:], l1b)
                p.dma_wait_all(s)

        def apply_v2(v, c, psMu, psRs):
            pass

        def apply_sync(sy, c, psMu, psRs):
            pass

        for nm in ["ln4c"]:
            pass

        xaff = p.sb("xaff4b", [128, 512], scope=sc)

        def apply_v3(v, c, psMu, psRs):
            for k in range(8):
                xs = xt[:, k * TL + c * 512:k * TL + c * 512 + 512]
                v.tensor_tensor(xscr[:], xs, psMu[:], ALU.subtract)
                v.tensor_tensor(xscr[:], xscr[:], psRs[:], ALU.mult)
                v.tensor_scalar(xaff[:], xscr[:], wsb[:, k:k + 1], bsb[:, k:k + 1],
                                ALU.mult, ALU.add)
                v.tensor_copy(xbf[:], xaff[:])
                v.tensor_copy(xs, xaff[:])

        def apply_sync3(sy, c, psMu, psRs):
            for k in range(8):
                pass

        with p.scope() as lnsc:
            emit_ln_apply(p, xt, TL, {'v': apply_v3}, "ln4c", scope=lnsc)

        with nc.Block() as blk:
            @blk.sync
            def _(s):
                for k in range(8):
                    p.dma(s, xn5_o[k * 128:(k + 1) * 128, :], xt[:, k * TL:(k + 1) * TL], q="st2")
                p.dma_wait_all(s, q="st2")
        xbfull = p.sb("xbfull", [128, 8 * TL], BF16, scope=sc)
        with nc.Block() as blk:
            @blk.vector
            def _(v):
                for k in range(8):
                    v.tensor_copy(xbfull[:, k * TL:(k + 1) * TL], xt[:, k * TL:(k + 1) * TL])
        with nc.Block() as blk:
            @blk.sync
            def _(s):
                for k in range(8):
                    p.dma(s, xn5b_o[k * 128:(k + 1) * 128, :], xbfull[:, k * TL:(k + 1) * TL], q="st2")
                p.dma_wait_all(s, q="st2")

    return p.done()


def host_phase4b_inputs(inputs, x4s):
    w1 = np.asarray(inputs["L0_ffn_w1"], np.float32)
    w3 = np.asarray(inputs["L0_ffn_w3"], np.float32)
    w2 = np.asarray(inputs["L0_ffn_w2"], np.float32)
    maps = []
    for c in range(NC):
        maps.append({
            "x4_fm": x4s[c],
            "w1c": np.ascontiguousarray(w1[c]),
            "w3c": np.ascontiguousarray(w3[c]),
            "w2c": np.ascontiguousarray(w2[c]),
            "l1w": vcols(inputs["L1_attn_a_ln1_w"]),
            "l1b": vcols(inputs["L1_attn_a_ln1_b"]),
        })
    return maps



TGF = 2 * TG


def build_phase5():
    p = Phase("ph5")
    nc = p.nc
    xnfull = p.din("xnfull", [D, TGF], BF16)
    winsl = [p.din(f"winsl{d}", [D, 512]) for d in range(2)]
    winxp = [p.din(f"winxp{d}", [D, 2048]) for d in range(2)]
    wxpT = [p.din(f"wxpT{d}", [2048, 192]) for d in range(2)]
    cw = [p.din(f"cw{d}", [128, 16 * 4]) for d in range(2)]
    cb = [p.din(f"cb{d}", [128, 16]) for d in range(2)]
    xnown = [p.din(f"xnown{d}", [D, 2 * 1028], BF16) for d in range(2)]

    xpz_o = [p.dout(f"xpz{d}", [512, TGF], BF16) for d in range(2)]
    xdbl_o = [p.dout(f"xdbl{d}", [192, TL]) for d in range(2)]

    with p.scope() as sc:
        wsl = [p.sb(f"wslb{d}", [128, 8 * 512], BF16, scope=sc) for d in range(2)]
        xch = [p.sb(f"xch{i}", [128, 8 * 512], BF16, scope=sc) for i in range(2)]
        ot = p.sb("ot5", [128, 512], BF16, scope=sc)
        psz = [p.ps([128, 512], scope=sc) for _ in range(4)]

        with nc.Block() as blk:
            @blk.gpsimd
            def _(g):
                for d in range(2):
                    for k in range(8):
                        p.dma(g, wsl[d][:, k * 512:(k + 1) * 512],
                              winsl[d][k * 128:(k + 1) * 128, :], q="gld")
                p.dma_wait_all(g, q="gld")

        for i in range(32):
            bb, ib = divmod(i, 16)
            with nc.Block() as blk:
                @blk.sync
                def _(s, i=i):
                    for k in range(8):
                        p.dma(s, xch[i % 2][:, k * 512:(k + 1) * 512],
                              xnfull[k * 128:(k + 1) * 128, i * 512:(i + 1) * 512])
                    p.dma_wait_all(s)
            for d in range(2):
                if d == 0:
                    ocol = i * 512
                    rev = False
                else:
                    ocol = (bb * 16 + (15 - ib)) * 512
                    rev = True
                with nc.Block() as blk:
                    @blk.tensor
                    def _(t, i=i, d=d, rev=rev):
                        for mt in range(4):
                            for k in range(8):
                                rhs = xch[i % 2][:, k * 512:(k + 1) * 512]
                                if rev:
                                    rhs = rhs[:, ::-1]
                                t.matmul(psz[mt][:],
                                         wsl[d][:, k * 512 + mt * 128:k * 512 + (mt + 1) * 128],
                                         rhs, start=(k == 0), stop=(k == 7))
                for mt in range(4):
                    with nc.Block() as blk:
                        if mt >= 2:
                            @blk.scalar
                            def _(s, d=d, mt=mt):
                                s.activation(ot[:], psz[mt][:], AF.Silu)
                        else:
                            @blk.vector
                            def _(v, d=d, mt=mt):
                                v.tensor_copy(ot[:], psz[mt][:])
                    with nc.Block() as blk:
                        @blk.sync
                        def _(s, d=d, mt=mt, ocol=ocol):
                            p.dma(s, xpz_o[d][mt * 128:(mt + 1) * 128, ocol:ocol + 512], ot[:], q="st")
                            p.dma_wait_all(s, q="st")

    for d in range(2):
        with p.scope() as sc:
            wxb = p.sb(f"wxb5_{d}", [128, 8 * 2048], BF16, scope=sc)
            wxpb = p.sb(f"wxpb5_{d}", [128, 16 * 192], BF16, scope=sc)
            cwsb = p.sb(f"cwsb_{d}", [128, 64], scope=sc)
            cbsb = p.sb(f"cbsb_{d}", [128, 16], scope=sc)
            xno = p.sb(f"xno5_{d}", [128, 8 * 2 * 1028], BF16, scope=sc)
            xpown = p.sb(f"xpown_{d}", [128, 16 * 1028], BF16, scope=sc)
            xcb = p.sb(f"xcb5_{d}", [128, 16 * 1024], BF16, scope=sc)
            acc = p.sb(f"acc5_{d}", [128, 1024], scope=sc)
            od = p.sb(f"od5_{d}", [128, 512], scope=sc)
            psd = [p.ps([128, 512], scope=sc) for _ in range(3)]

            with nc.Block() as blk:
                @blk.gpsimd
                def _(g, d=d):
                    for k in range(8):
                        p.dma(g, wxb[:, k * 2048:(k + 1) * 2048],
                              winxp[d][k * 128:(k + 1) * 128, :], q="gld")
                    for k in range(16):
                        p.dma(g, wxpb[:, k * 192:(k + 1) * 192],
                              wxpT[d][k * 128:(k + 1) * 128, :], q="gld")
                    p.dma(g, cwsb[:], cw[d], q="gld")
                    p.dma(g, cbsb[:], cb[d], q="gld")
                    p.dma_wait_all(g, q="gld")

                @blk.sync
                def _(s, d=d):
                    for k in range(8):
                        p.dma(s, xno[:, k * 2056:(k + 1) * 2056], xnown[d][k * 128:(k + 1) * 128, :])
                    p.dma_wait_all(s)

            for b in range(2):
                boff = b * 1028
                chs = [(0, 512), (512, 512), (1024, 4)]
                for mt in range(16):
                    with nc.Block() as blk:
                        @blk.tensor
                        def _(t, mt=mt, b=b, boff=boff):
                            for ci, (c0, cl) in enumerate(chs):
                                for k in range(8):
                                    t.matmul(psd[ci][:, 0:cl],
                                             wxb[:, k * 2048 + mt * 128:k * 2048 + (mt + 1) * 128],
                                             xno[:, k * 2056 + boff + c0:k * 2056 + boff + c0 + cl],
                                             start=(k == 0), stop=(k == 7))
                    with nc.Block() as blk:
                        @blk.vector
                        def _(v, mt=mt):
                            for ci, (c0, cl) in enumerate(chs):
                                v.tensor_copy(xpown[:, mt * 1028 + c0:mt * 1028 + c0 + cl],
                                              psd[ci][:, 0:cl])
                for mt in range(16):
                    with nc.Block() as blk:
                        @blk.vector
                        def _(v, mt=mt):
                            base = mt * 1028
                            v.tensor_scalar(acc[:], xpown[:, base + 1:base + 1 + 1024],
                                            cwsb[:, mt * 4:mt * 4 + 1], None, ALU.mult)
                            for j in range(1, 4):
                                v.scalar_tensor_tensor(acc[:], xpown[:, base + 1 + j:base + 1 + j + 1024],
                                                       cwsb[:, mt * 4 + j:mt * 4 + j + 1], acc[:],
                                                       ALU.mult, ALU.add)
                    with nc.Block() as blk:
                        @blk.scalar
                        def _(s, mt=mt):
                            s.activation(xcb[:, mt * 1024:(mt + 1) * 1024], acc[:],
                                         AF.Silu, bias=cbsb[:, mt:mt + 1])
                for mt2 in range(2):
                    m0 = mt2 * 128
                    msz = 128 if mt2 == 0 else 64
                    for ch in range(2):
                        with nc.Block() as blk:
                            @blk.tensor
                            def _(t, mt2=mt2, ch=ch, msz=msz):
                                for k in range(16):
                                    t.matmul(psd[ch][0:msz, :],
                                             wxpb[:, k * 192 + mt2 * 128:k * 192 + mt2 * 128 + msz],
                                             xcb[:, k * 1024 + ch * 512:k * 1024 + ch * 512 + 512],
                                             start=(k == 0), stop=(k == 15))
                        with nc.Block() as blk:
                            @blk.vector
                            def _(v, ch=ch, msz=msz):
                                v.tensor_copy(od[0:msz, :], psd[ch][0:msz, :])
                        with nc.Block() as blk:
                            @blk.sync
                            def _(s, d=d, b=b, mt2=mt2, ch=ch, msz=msz, m0=m0):
                                p.dma(s, xdbl_o[d][m0:m0 + msz, b * TB + ch * 512:b * TB + (ch + 1) * 512],
                                      od[0:msz, :], q="st")
                                p.dma_wait_all(s, q="st")

    with nc.Block() as blk:
        @blk.sync
        def _(s):
            p.dma_wait_all(s, q="st")

    return p.done()


def host_concat_fm_bf(slices):
    out = np.empty((D, TGF), BF)
    for c, sl in enumerate(slices):
        for b in range(B):
            out[:, b * TG + c * TB:b * TG + (c + 1) * TB] = np.asarray(sl)[:, b * TB:(b + 1) * TB]
    return np.ascontiguousarray(out)


def host_phase5_inputs(inputs, xn5b_slices):
    xnfull = host_concat_fm_bf(xn5b_slices)
    W1 = np.asarray(inputs["L1_attn_a_m1_W_in"], np.float32)
    W2 = np.asarray(inputs["L1_attn_a_m2_W_in"], np.float32)
    Wxp1 = np.asarray(inputs["L1_attn_a_m1_W_xp"], np.float32)
    Wxp2 = np.asarray(inputs["L1_attn_a_m2_W_xp"], np.float32)
    cw1 = np.asarray(inputs["L1_attn_a_m1_conv_w"], np.float32)
    cw2 = np.asarray(inputs["L1_attn_a_m2_conv_w"], np.float32)
    cb1 = np.asarray(inputs["L1_attn_a_m1_conv_b"], np.float32)
    cb2 = np.asarray(inputs["L1_attn_a_m2_conv_b"], np.float32)

    def cwcols(cwm):
        out = np.zeros((128, 64), np.float32)
        for mt in range(16):
            out[:, mt * 4:(mt + 1) * 4] = cwm[mt * 128:(mt + 1) * 128, :]
        return np.ascontiguousarray(out)

    def cbcols(cbm):
        return vcols(cbm)

    maps = []
    for c in range(NC):
        lo = c * 256
        m = {
            "xnfull": xnfull,
            "winsl0": np.ascontiguousarray(np.concatenate([W1[lo:lo + 256].T, W1[2048 + lo:2048 + lo + 256].T], 1)),
            "winsl1": np.ascontiguousarray(np.concatenate([W2[lo:lo + 256].T, W2[2048 + lo:2048 + lo + 256].T], 1)),
            "winxp0": np.ascontiguousarray(W1[:2048].T),
            "winxp1": np.ascontiguousarray(W2[:2048].T),
            "wxpT0": np.ascontiguousarray(Wxp1.T),
            "wxpT1": np.ascontiguousarray(Wxp2.T),
            "cw0": cwcols(cw1), "cw1": cwcols(cw2),
            "cb0": cbcols(cb1), "cb1": cbcols(cb2),
        }
        xo1 = np.zeros((D, 2 * 1028), BF)
        xo2 = np.zeros((D, 2 * 1028), BF)
        for b in range(B):
            t0 = c * TB
            own = xnfull[:, b * TG + t0:b * TG + t0 + TB]
            if t0 >= 4:
                halo1 = xnfull[:, b * TG + t0 - 4:b * TG + t0]
            else:
                halo1 = np.zeros((D, 4), BF)
                if t0 > 0:
                    halo1[:, 4 - t0:] = xnfull[:, b * TG:b * TG + t0]
            xo1[:, b * 1028:b * 1028 + 4] = halo1
            xo1[:, b * 1028 + 4:(b + 1) * 1028] = own
            own_rev = own[:, ::-1]
            t1 = t0 + TB
            h2 = np.zeros((D, 4), BF)
            nav = min(4, TG - t1)
            if nav > 0:
                seg = xnfull[:, b * TG + t1:b * TG + t1 + nav]
                h2[:, 4 - nav:] = seg[:, ::-1]
            xo2[:, b * 1028:b * 1028 + 4] = h2
            xo2[:, b * 1028 + 4:(b + 1) * 1028] = own_rev
        m["xnown0"] = np.ascontiguousarray(xo1)
        m["xnown1"] = np.ascontiguousarray(xo2)
        maps.append(m)
    return maps



def build_phase6():
    p = Phase("ph6")
    nc = p.nc
    xpz = [p.din(f"xpz{d}", [512, TGF], BF16) for d in range(2)]
    xdbl = [p.din(f"xdblf{d}", [192, TGF]) for d in range(2)]
    wdtT = [p.din(f"wdtT{d}", [64, 256]) for d in range(2)]
    alog = [p.din(f"alog{d}", [128, 128]) for d in range(2)]
    dtb = [p.din(f"dtb{d}", [128, 2]) for d in range(2)]
    dcol = [p.din(f"dcol{d}", [128, 2]) for d in range(2)]
    cwsl = [p.din(f"cwsl{d}", [128, 2 * 4]) for d in range(2)]
    cbsl = [p.din(f"cbsl{d}", [128, 2]) for d in range(2)]

    y_o = [p.dout(f"y{d}", [256, TGF], BF16) for d in range(2)]

    a2 = p.sb("a2", [128, 2 * 128])
    wdt = p.sb("wdt", [64, 2 * 256])
    dtbc = p.sb("dtbc", [128, 4])
    dcc = p.sb("dcc", [128, 4])
    cwc = p.sb("cwc", [128, 16])
    cbc = p.sb("cbc", [128, 4])

    with nc.Block() as blk:
        @blk.gpsimd
        def _(g):
            for d in range(2):
                p.dma(g, wdt[:, d * 256:(d + 1) * 256], wdtT[d], q="gld")
                p.dma(g, a2[:, d * 128:(d + 1) * 128], alog[d], q="gld")
                p.dma(g, dtbc[:, d * 2:(d + 1) * 2], dtb[d], q="gld")
                p.dma(g, dcc[:, d * 2:(d + 1) * 2], dcol[d], q="gld")
                p.dma(g, cwc[:, d * 8:(d + 1) * 8], cwsl[d], q="gld")
                p.dma(g, cbc[:, d * 2:(d + 1) * 2], cbsl[d], q="gld")
            p.dma_wait_all(g, q="gld")
    with nc.Block() as blk:
        @blk.scalar
        def _(s):
            s.activation(a2[:], a2[:], AF.Exp)
    with nc.Block() as blk:
        @blk.vector
        def _(v):
            v.tensor_single_scalar(a2[:], a2[:], -1.0, ALU.mult)

    dtbf = p.sb("dtbf", [128, TG], BF16)
    xpad = p.sb("xpad", [128, TG + 4], BF16)
    ubf = xpad[:, 0:TG]
    xcb = p.sb("xcb6", [128, TG], BF16)
    wbf = p.sb("wbf", [128, TG], BF16)
    dabf = [p.sb(f"dabf{i}", [128, TG], BF16) for i in range(2)]
    bbc = [p.sb(f"bbc{i}", [128, TG], BF16) for i in range(2)]
    ccb = [p.sb(f"ccb{i}", [128, TG], BF16) for i in range(2)]
    ysf = p.sb("ysf", [128, TG])
    sp1 = ysf[:, 0:512]
    sp2 = ysf[:, 512:1024]
    psdt = p.ps([128, 512])

    bq = p.sem("bq")
    cq = p.sem("cq")
    ssem = p.sem("ssem")
    vsem = p.sem("vsem")

    import os as _os
    _g6 = int(_os.environ.get("P6_GROUPS", "8"))
    _gi = 0
    for d in range(2):
        for eb in range(2):
            for b in range(2):
                _gi += 1
                if _os.environ.get("P6_ONLY"):
                    if _gi != int(_os.environ["P6_ONLY"]):
                        continue
                elif _gi > _g6:
                    continue
                dtlo = ysf[0:64, 0:TG]
                with nc.Block() as blk:
                    @blk.sync
                    def _(s, d=d, b=b):
                        p.dma(s, dtlo, xdbl[d][0:64, b * TG:(b + 1) * TG])
                        p.dma_wait_all(s)
                for ch in range(16):
                    with nc.Block() as blk:
                        @blk.tensor
                        def _(t, d=d, eb=eb, b=b, ch=ch):
                            t.matmul(psdt[:], wdt[:, d * 256 + eb * 128:d * 256 + (eb + 1) * 128],
                                     ysf[0:64, ch * 512:(ch + 1) * 512],
                                     start=True, stop=True)
                    with nc.Block() as blk:
                        @blk.vector
                        def _(v, d=d, eb=eb):
                            v.tensor_scalar(sp2[:], psdt[:], dtbc[:, d * 2 + eb:d * 2 + eb + 1],
                                            None, ALU.add)
                    with nc.Block() as blk:
                        @blk.scalar
                        def _(s):
                            s.activation(sp1[:], sp2[:], AF.Abs)
                    with nc.Block() as blk:
                        @blk.scalar
                        def _(s):
                            s.activation(sp1[:], sp1[:], AF.Exp, scale=-1.0)
                            s.activation(sp1[:], sp1[:], AF.Ln, bias=1.0)

                        @blk.vector
                        def _(v):
                            v.tensor_relu(sp2[:], sp2[:])
                    with nc.Block() as blk:
                        @blk.vector
                        def _(v, ch=ch):
                            v.tensor_add(dtbf[:, ch * 512:(ch + 1) * 512], sp2[:], sp1[:])

                with nc.Block() as blk:
                    @blk.gpsimd
                    def _(g):
                        g.memset(xpad[:, 0:4], 0.0)

                    @blk.sync
                    def _(s, d=d, eb=eb, b=b):
                        p.dma(s, xpad[:, 4:TG + 4],
                              xpz[d][eb * 128:(eb + 1) * 128, b * TG:(b + 1) * TG])
                        p.dma_wait_all(s)
                for ch in range(16):
                    with nc.Block() as blk:
                        @blk.vector
                        def _(v, d=d, eb=eb, ch=ch):
                            base = ch * 512
                            v.tensor_scalar(sp1[:], xpad[:, base + 1:base + 513],
                                            cwc[:, (d * 2 + eb) * 4:(d * 2 + eb) * 4 + 1], None, ALU.mult)
                            for j in range(1, 4):
                                v.scalar_tensor_tensor(sp1[:], xpad[:, base + 1 + j:base + 513 + j],
                                                       cwc[:, (d * 2 + eb) * 4 + j:(d * 2 + eb) * 4 + j + 1],
                                                       sp1[:], ALU.mult, ALU.add)
                    with nc.Block() as blk:
                        @blk.scalar
                        def _(s, d=d, eb=eb, ch=ch):
                            s.activation(xcb[:, ch * 512:(ch + 1) * 512], sp1[:],
                                         AF.Silu, bias=cbc[:, d * 2 + eb:d * 2 + eb + 1])
                with nc.Block() as blk:
                    @blk.vector
                    def _(v):
                        v.tensor_mul(wbf[:], dtbf[:], xcb[:])

                    @blk.gpsimd
                    def _(g):
                        g.memset(ysf[:], 0.0)

                bq0, cq0, ss0, vs0 = bq[1], cq[1], ssem[1], vsem[1]
                with nc.Block() as blk:
                    @blk.gpsimd
                    def _(g, d=d, b=b, vs0=vs0):
                        for n in range(64):
                            if n >= 2:
                                g.wait_ge(vsem[0], vs0 + n - 1)
                            srcb = xdbl[d][64 + n:65 + n, b * TG:(b + 1) * TG].partition_broadcast(128).squeeze(1)
                            p.dma(g, bbc[n % 2][:], srcb, q="bq")
                            srcc = xdbl[d][128 + n:129 + n, b * TG:(b + 1) * TG].partition_broadcast(128).squeeze(1)
                            p.dma(g, ccb[n % 2][:], srcc, q="cq")

                    @blk.scalar
                    def _(s, d=d, eb=eb, vs0=vs0):
                        for n in range(64):
                            if n >= 2:
                                s.wait_ge(vsem[0], vs0 + n - 1)
                            col = d * 128 + eb * 64 + n
                            inst = s.activation(dabf[n % 2][:], dtbf[:], AF.Exp,
                                                scale=a2[:, col:col + 1])
                            inst.then_inc(ssem[0], 1)

                    @blk.vector
                    def _(v, bq0=bq0, cq0=cq0, ss0=ss0):
                        for n in range(64):
                            v.wait_ge(bq[0], bq0 + 16 * (n + 1))
                            v.wait_ge(cq[0], cq0 + 16 * (n + 1))
                            v.wait_ge(ssem[0], ss0 + n + 1)
                            v.tensor_mul(ubf, wbf[:], bbc[n % 2][:])
                            v.tensor_tensor_scan(bbc[n % 2][:], dabf[n % 2][:], ubf, 0.0,
                                                 ALU.mult, ALU.add)
                            v.tensor_mul(ubf, bbc[n % 2][:], ccb[n % 2][:])
                            inst = v.tensor_add(ysf[:], ysf[:], ubf)
                            inst.then_inc(vsem[0], 1)
                ssem[1] += 64
                vsem[1] += 64

                with nc.Block() as blk:
                    @blk.sync
                    def _(s, d=d, eb=eb, b=b):
                        p.dma(s, wbf[:], xpz[d][256 + eb * 128:256 + (eb + 1) * 128,
                                                b * TG:(b + 1) * TG])
                        p.dma_wait_all(s)
                with nc.Block() as blk:
                    @blk.vector
                    def _(v, d=d, eb=eb):
                        v.scalar_tensor_tensor(ysf[:], xcb[:], dcc[:, d * 2 + eb:d * 2 + eb + 1],
                                               ysf[:], ALU.mult, ALU.add)
                        v.tensor_mul(dabf[0][:], ysf[:], wbf[:])
                with nc.Block() as blk:
                    @blk.sync
                    def _(s, d=d, eb=eb, b=b):
                        p.dma(s, y_o[d][eb * 128:(eb + 1) * 128, b * TG:(b + 1) * TG],
                              dabf[0][:], q="st")
                        p.dma_wait_all(s, q="st")

    with nc.Block() as blk:
        @blk.sync
        def _(s):
            p.dma_wait_all(s, q="st")

    return p.done()


def host_phase6_inputs(inputs, r5s):
    xdblf = []
    for d in range(2):
        full = np.zeros((192, TGF), np.float32)
        for c in range(NC):
            sl = r5s[c][f"xdbl{d}"]
            for b in range(B):
                if d == 0:
                    t0 = c * TB
                else:
                    t0 = (NC - 1 - c) * TB
                full[:, b * TG + t0:b * TG + t0 + TB] = sl[:, b * TB:(b + 1) * TB]
        xdblf.append(np.ascontiguousarray(full))
    maps = []
    for c in range(NC):
        lo = c * 256
        m = {}
        for d in range(2):
            pre = f"m{d+1}_"
            m[f"xpz{d}"] = r5s[c][f"xpz{d}"]
            m[f"xdblf{d}"] = xdblf[d]
            m[f"wdtT{d}"] = np.ascontiguousarray(
                np.asarray(inputs[f"L1_attn_a_{pre[:-1]}_W_dt"], np.float32).T[:, lo:lo + 256])
            al = np.asarray(inputs[f"L1_attn_a_{pre[:-1]}_A_log"], np.float32)[lo:lo + 256]
            m[f"alog{d}"] = np.ascontiguousarray(
                np.concatenate([al[:128], al[128:]], axis=1))
            m[f"dtb{d}"] = np.ascontiguousarray(np.stack(
                [np.asarray(inputs[f"L1_attn_a_{pre[:-1]}_dt_b"], np.float32)[lo:lo + 128],
                 np.asarray(inputs[f"L1_attn_a_{pre[:-1]}_dt_b"], np.float32)[lo + 128:lo + 256]], 1))
            m[f"dcol{d}"] = np.ascontiguousarray(np.stack(
                [np.asarray(inputs[f"L1_attn_a_{pre[:-1]}_D"], np.float32)[lo:lo + 128],
                 np.asarray(inputs[f"L1_attn_a_{pre[:-1]}_D"], np.float32)[lo + 128:lo + 256]], 1))
            cwm = np.asarray(inputs[f"L1_attn_a_{pre[:-1]}_conv_w"], np.float32)[lo:lo + 256]
            m[f"cwsl{d}"] = np.ascontiguousarray(np.concatenate([cwm[:128], cwm[128:]], 1))
            cbm = np.asarray(inputs[f"L1_attn_a_{pre[:-1]}_conv_b"], np.float32)[lo:lo + 256]
            m[f"cbsl{d}"] = np.ascontiguousarray(np.stack([cbm[:128], cbm[128:]], 1))
        maps.append(m)
    return maps



def build_phase7():
    p = Phase("ph7")
    nc = p.nc
    y1 = p.din("y1own", [2048, TL], BF16)
    y2 = p.din("y2own", [2048, TL], BF16)
    wo1 = p.din("wo1T", [2048, D])
    wo2 = p.din("wo2T", [2048, D])
    l2w = p.din("l2w", [128, 8])
    l2b = p.din("l2b", [128, 8])
    x5 = p.din("x5_fm", [D, TL])
    xn5a = p.din("xn5a_fm", [D, TL])
    fw = p.din("f1w", [D, D])
    fa1 = p.din("f1a1", [D, D])
    fa2 = p.din("f1a2", [D, D])
    fb = p.din("f1b", [128, 8])
    qkvT = [p.din(f"qkvT{s}", [D, D]) for s in range(3)]
    rcos = p.din("rcos", [64, TL])
    rsin = p.din("rsin", [64, TL])
    kpos = p.din("kposf", [128, 8 * 16])
    vpos = p.din("vposf", [128, 8 * 16])
    lam = p.din("lam", [1, 1])
    kfcT = p.din("kfcT", [2048, 8192])
    vfcT = p.din("vfcT", [2048, 8192])
    kpjT = p.din("kpjT", [8192, 128])
    vpjT = p.din("vpjT", [8192, 128])
    swT = p.din("swT", [D, 24])
    sbcol = p.din("sbcol", [24, 1])

    x7_o = p.dout("x7_fm", [D, TL])
    q_o = p.dout("q_rot", [1024, TL], BF16)
    k_o = p.dout("k_rot", [1024, TL], BF16)
    v_o = p.dout("v_sc", [1024, TL], BF16)
    ck_o = p.dout("ck", [128, 16 * 64], BF16)
    cv_o = p.dout("cv", [128, 16 * 64], BF16)
    strat_o = p.dout("strat", [24, TL])

    xn7 = p.sb("xn7", [128, 8 * TL], BF16)
    xscr = p.sb("xscr7", [128, 512])
    xscr2 = p.sb("xscr7b", [128, 512])
    l2wc = p.sb("l2wc", [128, 8])
    l2bc = p.sb("l2bc", [128, 8])
    sc_ssb = p.scope()
    ssb = p.sb("ssb7", [128, 8 * TL], scope=sc_ssb)

    for di, (yy, ww) in enumerate(((y1, wo1), (y2, wo2))):
        with p.scope() as sc:
            yb = p.sb(f"yb7_{di}", [128, 16 * TL], BF16, scope=sc)
            wb = p.sb(f"wb7_{di}", [128, 16 * 1024], BF16, scope=sc)
            pso = [p.ps([128, 512], scope=sc) for _ in range(4)]
            with nc.Block() as blk:
                @blk.sync
                def _(s, yy=yy, yb=yb):
                    for k in range(16):
                        p.dma(s, yb[:, k * TL:(k + 1) * TL], yy[k * 128:(k + 1) * 128, :])
                    p.dma_wait_all(s)

                @blk.gpsimd
                def _(g, ww=ww, wb=wb):
                    for k in range(16):
                        p.dma(g, wb[:, k * 1024:(k + 1) * 1024], ww[k * 128:(k + 1) * 128, :], q="gld")
                    p.dma_wait_all(g, q="gld")
            for mt in range(8):
                with nc.Block() as blk:
                    @blk.tensor
                    def _(t, mt=mt, yb=yb, wb=wb):
                        for ch in range(4):
                            for k in range(16):
                                t.matmul(pso[ch][:], wb[:, k * 1024 + mt * 128:k * 1024 + (mt + 1) * 128],
                                         yb[:, k * TL + ch * 512:k * TL + ch * 512 + 512],
                                         start=(k == 0), stop=(k == 15))
                with nc.Block() as blk:
                    @blk.vector
                    def _(v, mt=mt, di=di):
                        for ch in range(4):
                            sl = ssb[:, mt * TL + ch * 512:mt * TL + ch * 512 + 512]
                            if di == 0:
                                v.tensor_copy(sl, pso[ch][:])
                            else:
                                v.tensor_add(sl, sl, pso[ch][:])

    with nc.Block() as blk:
        @blk.sync
        def _(s):
            p.dma(s, l2wc[:], l2w)
            p.dma(s, l2bc[:], l2b)
            p.dma_wait_all(s)

    def apply2(v, c, psMu, psRs):
        for k in range(8):
            xs = ssb[:, k * TL + c * 512:k * TL + c * 512 + 512]
            v.tensor_tensor(xscr[:], xs, psMu[:], ALU.subtract)
            v.tensor_tensor(xscr[:], xscr[:], psRs[:], ALU.mult)
            v.tensor_scalar(xs, xscr[:], l2wc[:, k:k + 1], l2bc[:, k:k + 1], ALU.mult, ALU.add)

    with p.scope() as lnsc:
        emit_ln_apply(p, ssb, TL, {'v': apply2}, "ln7a", scope=lnsc)

    for mt in range(8):
        for ch in range(4):
            with nc.Block() as blk:
                @blk.sync
                def _(s, mt=mt, ch=ch):
                    p.dma(s, xscr[:], x5[mt * 128:(mt + 1) * 128, ch * 512:(ch + 1) * 512])
                    p.dma(s, xscr2[:], xn5a[mt * 128:(mt + 1) * 128, ch * 512:(ch + 1) * 512])
                    p.dma_wait_all(s)
            with nc.Block() as blk:
                @blk.vector
                def _(v, mt=mt, ch=ch):
                    sl = ssb[:, mt * TL + ch * 512:mt * TL + ch * 512 + 512]
                    v.tensor_add(sl, sl, xscr[:])
                    v.tensor_add(sl, sl, xscr2[:])

    sc_xnf = p.scope()
    xnf = p.sb("xnf7", [128, 8 * TL], BF16, scope=sc_xnf)

    def apply3(v, c, psMu, psRs):
        for k in range(8):
            xs = ssb[:, k * TL + c * 512:k * TL + c * 512 + 512]
            xn = xnf[:, k * TL + c * 512:k * TL + c * 512 + 512]
            v.tensor_tensor(xscr[:], xs, psMu[:], ALU.subtract)
            v.tensor_tensor(xn, xscr[:], psRs[:], ALU.mult)

    with p.scope() as lnsc:
        emit_ln_apply(p, ssb, TL, {'v': apply3}, "ln7b", scope=lnsc)

    with p.scope() as sc:
        wbuf = p.sb("fwb", [128, 8 * 1024], BF16, scope=sc)
        fbc = p.sb("fbc", [128, 8], scope=sc)
        h1 = p.sb("h1b", [128, 8 * TL], BF16, scope=sc)
        ps4 = [p.ps([128, 512], scope=sc) for _ in range(4)]
        with nc.Block() as blk:
            @blk.sync
            def _(s):
                p.dma(s, fbc[:], fb)
                p.dma_wait_all(s)
        for step, (wsrc, xin, xout) in enumerate(((fw, xnf, h1), (fa1, h1, xnf), (fa2, xnf, h1))):
            with nc.Block() as blk:
                @blk.gpsimd
                def _(g, wsrc=wsrc):
                    for k in range(8):
                        p.dma(g, wbuf[:, k * 1024:(k + 1) * 1024], wsrc[k * 128:(k + 1) * 128, :], q="gld")
                    p.dma_wait_all(g, q="gld")
            for mt in range(8):
                with nc.Block() as blk:
                    @blk.tensor
                    def _(t, xin=xin, mt=mt):
                        for ch in range(4):
                            for k in range(8):
                                t.matmul(ps4[ch][:], wbuf[:, k * 1024 + mt * 128:k * 1024 + (mt + 1) * 128],
                                         xin[:, k * TL + ch * 512:k * TL + ch * 512 + 512],
                                         start=(k == 0), stop=(k == 7))
                if step < 2:
                    with nc.Block() as blk:
                        @blk.vector
                        def _(v, xout=xout, mt=mt):
                            for ch in range(4):
                                v.tensor_copy(xout[:, mt * TL + ch * 512:mt * TL + ch * 512 + 512], ps4[ch][:])
                else:
                    with nc.Block() as blk:
                        @blk.scalar
                        def _(s, mt=mt):
                            for ch in range(4):
                                s.activation(h1[:, mt * TL + ch * 512:mt * TL + ch * 512 + 512],
                                             ps4[ch][:], AF.Tanh, bias=fbc[:, mt:mt + 1])
                    with nc.Block() as blk:
                        @blk.vector
                        def _(v, mt=mt):
                            for ch in range(4):
                                sl = ssb[:, mt * TL + ch * 512:mt * TL + ch * 512 + 512]
                                v.tensor_tensor(sl, sl, h1[:, mt * TL + ch * 512:mt * TL + ch * 512 + 512], ALU.add)

    sc_xnf.close()
    with nc.Block() as blk:
        @blk.sync
        def _(s):
            for k in range(8):
                p.dma(s, x7_o[k * 128:(k + 1) * 128, :], ssb[:, k * TL:(k + 1) * TL], q="st")
            p.dma_wait_all(s, q="st")

    def apply4(v, c, psMu, psRs):
        for k in range(8):
            xs = ssb[:, k * TL + c * 512:k * TL + c * 512 + 512]
            xn = xn7[:, k * TL + c * 512:k * TL + c * 512 + 512]
            v.tensor_tensor(xscr[:], xs, psMu[:], ALU.subtract)
            v.tensor_tensor(xn, xscr[:], psRs[:], ALU.mult)

    with p.scope() as lnsc:
        emit_ln_apply(p, ssb, TL, {'v': apply4}, "ln7c", scope=lnsc)
    sc_ssb.close()

    kpf = p.sb("kpf", [128, 8 * TL], BF16)
    vpf = p.sb("vpf", [128, 8 * TL], BF16)
    with p.scope() as sc:
        qkb = [p.sb(f"qkvb{s}", [128, 8 * 1024], BF16, scope=sc) for s in range(3)]
        cossb = p.sb("cossb", [64, TL], BF16, scope=sc)
        sinsb = p.sb("sinsb", [64, TL], BF16, scope=sc)
        kpsb = p.sb("kpsb", [128, 8 * 16], scope=sc)
        vpsb = p.sb("vpsb", [128, 8 * 16], scope=sc)
        lamc = p.sb("lamc", [128, 1], scope=sc)
        onr = p.sb("onr7", [1, 128], scope=sc)
        sc_qkv = p.scope()
        qt = p.sb("qt7", [128, TL], scope=sc_qkv)
        qhi = p.sb("qhi7", [64, TL], scope=sc_qkv)
        qs2 = p.sb("qs27", [64, TL], scope=sc_qkv)
        qs = p.sb("qs7", [128, TL], scope=sc_qkv)
        rr = p.sb("rr7", [1, TL], scope=sc_qkv)
        qr2 = p.sb("qr7", [128, TL], BF16, scope=sc_qkv)
        psq = [p.ps([128, 512], scope=sc_qkv) for _ in range(4)]
        psr = p.ps([1, 512], scope=sc_qkv)
        psb = p.ps([128, 512], scope=sc_qkv)
        psl = p.ps([128, 1], scope=sc_qkv, name="psl")

        with nc.Block() as blk:
            @blk.gpsimd
            def _(g):
                for s_ in range(3):
                    for k in range(8):
                        p.dma(g, qkb[s_][:, k * 1024:(k + 1) * 1024],
                              qkvT[s_][k * 128:(k + 1) * 128, :], q="gld")
                p.dma_wait_all(g, q="gld")
                g.memset(onr[:], 1.0)

            @blk.sync
            def _(s):
                p.dma(s, kpsb[:], kpos)
                p.dma(s, vpsb[:], vpos)
                p.dma_wait_all(s)
        with nc.Block() as blk:
            @blk.gpsimd
            def _(g):
                p.dma(g, cossb[:], rcos, q="gld")
                p.dma(g, sinsb[:], rsin, q="gld")
                p.dma_wait_all(g, q="gld")
        with nc.Block() as blk:
            @blk.sync
            def _(s):
                p.dma(s, rr[0:1, 0:1], lam)
                p.dma_wait_all(s)
        with nc.Block() as blk:
            @blk.tensor
            def _(t):
                t.matmul(psl[:], onr[:], rr[0:1, 0:1], start=True, stop=True)
        with nc.Block() as blk:
            @blk.vector
            def _(v):
                v.tensor_copy(lamc[:], psl[:])

        oneK7 = p.sb("oneK7", [128, 1], scope=sc_qkv)
        with nc.Block() as blk:
            @blk.gpsimd
            def _(g):
                g.memset(oneK7[:], 1.0)

        for s_ in range(3):
            for h in range(8):
                with nc.Block() as blk:
                    @blk.tensor
                    def _(t, s_=s_, h=h):
                        for ch in range(4):
                            for k in range(8):
                                t.matmul(psq[ch][:], qkb[s_][:, k * 1024 + h * 128:k * 1024 + (h + 1) * 128],
                                         xn7[:, k * TL + ch * 512:k * TL + ch * 512 + 512],
                                         start=(k == 0), stop=(k == 7))
                if s_ == 2:
                    with nc.Block() as blk:
                        @blk.vector
                        def _(v, h=h):
                            for ch in range(4):
                                v.tensor_scalar(qt[:, ch * 512:(ch + 1) * 512], psq[ch][:],
                                                lamc[:, 0:1], None, ALU.mult)
                    with nc.Block() as blk:
                        @blk.vector
                        def _(v, h=h):
                            v.tensor_copy(qr2[:], qt[:])
                            for b in range(2):
                                v.tensor_tensor(
                                    vpf[:, h * TL + b * TB:h * TL + (b + 1) * TB].rearrange("p (w s) -> p w s", s=16),
                                    qt[:, b * TB:(b + 1) * TB].rearrange("p (w s) -> p w s", s=16),
                                    vpsb[:, h * 16:(h + 1) * 16][:, None, :].broadcast_to([128, 64, 16]),
                                    ALU.add)
                    with nc.Block() as blk:
                        @blk.sync
                        def _(s, h=h):
                            p.dma(s, v_o[h * 128:(h + 1) * 128, :], qr2[:], q="st")
                            p.dma_wait_all(s, q="st")
                    continue
                with nc.Block() as blk:
                    @blk.vector
                    def _(v, h=h):
                        for ch in range(4):
                            v.tensor_copy(qt[:, ch * 512:(ch + 1) * 512], psq[ch][:])
                with nc.Block() as blk:
                    @blk.scalar
                    def _(s, h=h):
                        s.activation(qs[:], qt[:], AF.Square)
                for ch in range(4):
                    with nc.Block() as blk:
                        @blk.tensor
                        def _(t, ch=ch):
                            t.matmul(psr[:], oneK7[:], qs[:, ch * 512:(ch + 1) * 512],
                                     start=True, stop=True)
                    with nc.Block() as blk:
                        @blk.vector
                        def _(v, ch=ch):
                            v.tensor_scalar(rr[0:1, ch * 512:(ch + 1) * 512], psr[0:1, :],
                                            1.0 / 128.0, 1e-6, ALU.mult, ALU.add)
                with nc.Block() as blk:
                    @blk.scalar
                    def _(s):
                        s.activation(rr[0:1, :], rr[0:1, :], AF.Sqrt)
                with nc.Block() as blk:
                    @blk.vector
                    def _(v):
                        v.reciprocal(rr[0:1, :], rr[0:1, :])
                for ch in range(4):
                    with nc.Block() as blk:
                        @blk.tensor
                        def _(t, ch=ch):
                            t.matmul(psb[:], onr[:], rr[0:1, ch * 512:(ch + 1) * 512],
                                     start=True, stop=True)
                    with nc.Block() as blk:
                        @blk.vector
                        def _(v, ch=ch):
                            v.tensor_tensor(qt[:, ch * 512:(ch + 1) * 512],
                                            qt[:, ch * 512:(ch + 1) * 512], psb[:], ALU.mult)
                with nc.Block() as blk:
                    @blk.vector
                    def _(v):
                        v.tensor_copy(qhi[:, :], qt[64:128, :])
                        v.tensor_tensor(qs[0:64, :], qt[0:64, :], cossb[:], ALU.mult)
                        v.tensor_tensor(qs2[:, :], qhi[:, :], sinsb[:], ALU.mult)
                        v.tensor_tensor(qr2[0:64, :], qs[0:64, :], qs2[:, :], ALU.add)
                        v.tensor_tensor(qs[0:64, :], qhi[:, :], cossb[:], ALU.mult)
                        v.tensor_tensor(qs2[:, :], qt[0:64, :], sinsb[:], ALU.mult)
                        v.tensor_tensor(qs[0:64, :], qs[0:64, :], qs2[:, :], ALU.subtract)
                        v.tensor_copy(qr2[64:128, :], qs[0:64, :])
                with nc.Block() as blk:
                    @blk.sync
                    def _(s, s_=s_, h=h):
                        p.dma(s, (q_o if s_ == 0 else k_o)[h * 128:(h + 1) * 128, :], qr2[:], q="st")
                        p.dma_wait_all(s, q="st")
                if s_ == 1:
                    with nc.Block() as blk:
                        @blk.vector
                        def _(v, h=h):
                            for b in range(2):
                                v.tensor_tensor(
                                    kpf[:, h * TL + b * TB:h * TL + (b + 1) * TB].rearrange("p (w s) -> p w s", s=16),
                                    qr2[:, b * TB:(b + 1) * TB].rearrange("p (w s) -> p w s", s=16),
                                    kpsb[:, h * 16:(h + 1) * 16][:, None, :].broadcast_to([128, 64, 16]),
                                    ALU.add)

        sc_qkv.close()
        swb = p.sb("swb7", [128, 8 * 24], BF16, scope=sc)
        sbc = p.sb("sbc7", [24, 1], scope=sc)
        strt = p.sb("strt", [24, TL], scope=sc)
        pss = p.ps([24, 512], scope=sc, name="pss")
        with nc.Block() as blk:
            @blk.gpsimd
            def _(g):
                for k in range(8):
                    p.dma(g, swb[:, k * 24:(k + 1) * 24], swT[k * 128:(k + 1) * 128, :], q="gld")
                p.dma_wait_all(g, q="gld")

            @blk.sync
            def _(s):
                p.dma(s, sbc[:], sbcol)
                p.dma_wait_all(s)
        for ch in range(4):
            with nc.Block() as blk:
                @blk.tensor
                def _(t, ch=ch):
                    for k in range(8):
                        t.matmul(pss[:], swb[:, k * 24:(k + 1) * 24],
                                 xn7[:, k * TL + ch * 512:k * TL + ch * 512 + 512],
                                 start=(k == 0), stop=(k == 7))
            with nc.Block() as blk:
                @blk.scalar
                def _(s, ch=ch):
                    s.activation(strt[0:24, ch * 512:(ch + 1) * 512], pss[0:24, :],
                                 AF.Sigmoid, bias=sbc[0:24, 0:1])
        with nc.Block() as blk:
            @blk.sync
            def _(s):
                p.dma(s, strat_o[:, :], strt[0:24, :], q="st")
                p.dma_wait_all(s, q="st")

    for which, srcF, fcT, pjT, out_d in ((0, kpf, kfcT, kpjT, ck_o), (1, vpf, vfcT, vpjT, cv_o)):
        with p.scope() as sc:
            fstg = p.sb(f"fstg{which}", [128, 16 * 128], BF16, scope=sc)
            pjb = p.sb(f"pjb{which}", [128, 64 * 128], BF16, scope=sc)
            midb = p.sb(f"midb{which}", [128, 1024], BF16, scope=sc)
            mscr = p.sb(f"mscr{which}", [128, 1024], scope=sc)
            ckout = p.sb(f"ckout{which}", [128, 1024], BF16, scope=sc)
            psm = [p.ps([128, 512], scope=sc) for _ in range(2)]
            psck = [p.ps([128, 512], scope=sc) for _ in range(2)]

            with nc.Block() as blk:
                @blk.gpsimd
                def _(g, pjT=pjT, pjb=pjb):
                    for k in range(64):
                        p.dma(g, pjb[:, k * 128:(k + 1) * 128], pjT[k * 128:(k + 1) * 128, :], q="gld")
                    p.dma_wait_all(g, q="gld")

            for mt in range(64):
                with nc.Block() as blk:
                    @blk.gpsimd
                    def _(g, fcT=fcT, mt=mt, fstg=fstg):
                        for ks in range(16):
                            p.dma(g, fstg[:, ks * 128:(ks + 1) * 128],
                                  fcT[ks * 128:(ks + 1) * 128, mt * 128:(mt + 1) * 128], q="gld")
                        p.dma_wait_all(g, q="gld")
                with nc.Block() as blk:
                    @blk.tensor
                    def _(t, srcF=srcF, mt=mt, fstg=fstg):
                        for half in range(2):
                            for ks in range(16):
                                rhs = srcF.rearrange("p (h b w s) -> p h b w s", h=8, b=2, w=64, s=16)[
                                    :, half * 4:(half + 1) * 4, :, :, ks]
                                t.matmul(psm[half][:], fstg[:, ks * 128:(ks + 1) * 128],
                                         rhs, start=(ks == 0), stop=(ks == 15))
                with nc.Block() as blk:
                    @blk.vector
                    def _(v, mt=mt):
                        for half in range(2):
                            v.tensor_relu(mscr[:, half * 512:(half + 1) * 512], psm[half][:])
                with nc.Block() as blk:
                    @blk.scalar
                    def _(s, mt=mt):
                        s.activation(midb[:], mscr[:], AF.Square)
                with nc.Block() as blk:
                    @blk.tensor
                    def _(t, mt=mt, pjb=pjb):
                        for half in range(2):
                            t.matmul(psck[half][:], pjb[:, mt * 128:(mt + 1) * 128],
                                     midb[:, half * 512:(half + 1) * 512],
                                     start=(mt == 0), stop=(mt == 63))
            with nc.Block() as blk:
                @blk.vector
                def _(v):
                    for half in range(2):
                        v.tensor_copy(ckout[:, half * 512:(half + 1) * 512], psck[half][:])
            with nc.Block() as blk:
                @blk.sync
                def _(s, out_d=out_d, ckout=ckout):
                    p.dma(s, out_d[:, :], ckout[:, :], q="st")
                    p.dma_wait_all(s, q="st")

    return p.done()


def host_phase7_inputs(inputs, x5s, xn5s, y1own, y2own):
    qkvw = np.asarray(inputs["L1_attn_b_qkv_w"], np.float32)
    dn = 32
    ang = (1.0 / 1024.0) ** np.linspace(0.0, 1.0, dn)
    ang = np.concatenate([ang, np.zeros(64 - dn)])
    th = np.arange(TG, dtype=np.float64)[:, None] * ang[None, :]
    cosf = np.cos(th).astype(np.float32)
    sinf = np.sin(th).astype(np.float32)
    kp = np.asarray(inputs["L1_attn_b_k_pos"], np.float32)
    vp = np.asarray(inputs["L1_attn_b_v_pos"], np.float32)
    kposf = np.zeros((128, 8 * 16), np.float32)
    vposf = np.zeros((128, 8 * 16), np.float32)
    for h in range(8):
        kposf[:, h * 16:(h + 1) * 16] = kp[h].T
        vposf[:, h * 16:(h + 1) * 16] = vp[h].T
    maps = []
    for c in range(NC):
        t0 = c * TB
        rc = np.concatenate([cosf[t0:t0 + TB].T, cosf[t0:t0 + TB].T], 1)
        rs = np.concatenate([sinf[t0:t0 + TB].T, sinf[t0:t0 + TB].T], 1)
        maps.append({
            "y1own": y1own[c], "y2own": y2own[c],
            "wo1T": hT(inputs["L1_attn_a_m1_W_out"]),
            "wo2T": hT(inputs["L1_attn_a_m2_W_out"]),
            "l2w": vcols(inputs["L1_attn_a_ln2_w"]), "l2b": vcols(inputs["L1_attn_a_ln2_b"]),
            "x5_fm": x5s[c], "xn5a_fm": xn5s[c],
            "f1w": f32c(inputs["L1_ffn_weight"]),
            "f1a1": f32c(inputs["L1_ffn_aux_weight1"]),
            "f1a2": f32c(inputs["L1_ffn_aux_weight2"]),
            "f1b": vcols(inputs["L1_ffn_bias"]),
            "qkvT0": hT(qkvw[0]), "qkvT1": hT(qkvw[1]), "qkvT2": hT(qkvw[2]),
            "rcos": np.ascontiguousarray(rc), "rsin": np.ascontiguousarray(rs),
            "kposf": np.ascontiguousarray(kposf), "vposf": np.ascontiguousarray(vposf),
            "lam": np.asarray(inputs["L1_attn_b_lambdas"], np.float32)[:1].reshape(1, 1),
            "kfcT": hT(inputs["L1_attn_b_k_fc_w"]),
            "vfcT": hT(inputs["L1_attn_b_v_fc_w"]),
            "kpjT": hT(inputs["L1_attn_b_k_pj_w"]),
            "vpjT": hT(inputs["L1_attn_b_v_pj_w"]),
            "swT": hT(inputs["L1_attn_b_strat_w"]),
            "sbcol": np.ascontiguousarray(np.asarray(inputs["L1_attn_b_strat_b"], np.float32).reshape(24, 1)),
        })
    return maps


def f32c(x):
    return np.ascontiguousarray(np.asarray(x, np.float32))



def build_phase8():
    p = Phase("ph8")
    nc = p.nc
    q_my = p.din("q_my", [1024, 2048], BF16)
    kfull = p.din("kfull", [1024, TGF], BF16)
    vtm = p.din("vtm", [TGF, 1024], BF16)
    ksl = p.din("ksl", [1024, 16 * 256], BF16)
    vsl = p.din("vsl", [16 * 256, 1024], BF16)
    ckin_d = p.din("ckin", [128, 16 * 513], BF16)
    cvtm_d = p.din("cvtm", [16 * 513, 128], BF16)
    strat_d = p.din("strat_my", [128, 16 * 24])
    x7_d = p.din("x7_my", [1024, 2048])
    qbcol_d = p.din("qbcol", [128, 16])
    ownblk_d = p.din("ownblk", [128, 16 * 512])
    dij_d = p.din("dij", [128, 512])
    cmask_d = p.din("cmask", [128, 16 * 513])
    slmask_d = p.din("slmask", [128, 16 * 256])
    cprojT_d = p.din("cprojT", [1024, 1024])
    ident_in = p.din("ident_i", [128, 128], BF16)
    memk_d = p.din("memk", [8, 128])
    memv_d = p.din("memv", [1, 1024])

    x8_o = p.dout("x8_my", [1024, 2048])

    cks = p.sb("cks8", [128, 16 * 513], BF16)
    strat = p.sb("strat8", [128, 16 * 24])
    qbc = p.sb("qbc8", [128, 16])
    dij = p.sb("dij8", [128, 512])
    ident = p.sb("ident8", [128, 128], BF16)
    cpj = p.sb("cpj8", [128, 8 * 1024], BF16)
    with nc.Block() as blk:
        @blk.sync
        def _(s):
            p.dma(s, cks[:], ckin_d)
            p.dma(s, strat[:], strat_d)
            p.dma(s, qbc[:], qbcol_d)
            p.dma(s, dij[:], dij_d)
            p.dma(s, ident[:], ident_in)
            p.dma_wait_all(s)

        @blk.gpsimd
        def _(g):
            for k in range(8):
                p.dma(g, cpj[:, k * 1024:(k + 1) * 1024], cprojT_d[k * 128:(k + 1) * 128, :], q="gld")
            p.dma_wait_all(g, q="gld")

    with nc.Block() as blk:
        @blk.gpsimd
        def _(g):
            for s_ in range(16):
                h_ = s_ % 8
                p.dma(g, cks[:, s_ * 513:s_ * 513 + 1], memk_d[h_:h_ + 1, :], q="gld")
                p.dma(g, cvtm_d[s_ * 513:s_ * 513 + 1, :], memv_d[0:1, h_ * 128:(h_ + 1) * 128], q="gld")
            p.dma_wait_all(g, q="gld")

    import os as _os
    _only = _os.environ.get("P8_ONLY")
    _upto = int(_os.environ.get("P8_UPTO", "16"))
    for s_i in range(16):
        if _only is not None and s_i != int(_only):
            continue
        if s_i >= _upto:
            continue
        b = s_i // 8
        j = s_i % 8
        EXT = (j + 1) * 1024
        NCC = EXT // 512
        with p.scope() as sc:
            qslot = p.sb(f"qslot{s_i}", [128, 8 * 128], BF16, scope=sc)
            obk_s = p.sb(f"obks{s_i}", [128, 512], scope=sc)
            cmk_s = p.sb(f"cmks{s_i}", [128, 513], scope=sc)
            slm_s = p.sb(f"slms{s_i}", [128, 256], scope=sc)
            csb = p.sb(f"csb8_{s_i}", [128, 8 * 513], scope=sc)
            nmc = p.sb(f"nmc8_{s_i}", [128, 512], scope=sc)
            impb = p.sb(f"impb8_{s_i}", [128, 513], scope=sc)
            isb = p.sb(f"isb8_{s_i}", [128, 512], scope=sc)
            wrk = p.sb(f"wrk8_{s_i}", [128, 512], scope=sc)
            vals = p.sb(f"vals8_{s_i}", [128, 16], scope=sc)
            msel = p.sb(f"msel8_{s_i}", [128, 512], scope=sc)
            zz16 = p.sb(f"zz16_{s_i}", [128, 16], scope=sc)
            gts = p.sb(f"gts8_{s_i}", [128, 16], scope=sc)
            nmx = p.sb(f"nmx8_{s_i}", [128, 8], scope=sc)
            zrc = p.sb(f"zrc8_{s_i}", [128, 8], scope=sc)
            cvst = p.sb(f"cvst8_{s_i}", [128, 5 * 128], BF16, scope=sc)
            pcb = p.sb(f"pcb8_{s_i}", [128, 513], BF16, scope=sc)
            ptb = p.sb(f"ptb8_{s_i}", [128, 5 * 128], BF16, scope=sc)
            outb = p.sb(f"outb8_{s_i}", [128, 8 * 128], scope=sc)
            psc = [p.ps([128, 512], scope=sc) for _ in range(2)]
            psm1 = p.ps([128, 512], scope=sc, name=f"psm1_{s_i}")
            psav = p.ps([128, 512], scope=sc, name=f"psav8_{s_i}")
            pst2 = [p.ps([128, 512], BF16, scope=sc, name=f"pst8{i}_{s_i}") for i in range(2)]

            def pstv(i):
                return pst2[i // 4][:, (i % 4) * 128:(i % 4) * 128 + 128]

            with nc.Block() as blk:
                @blk.sync
                def _(s, s_i=s_i):
                    for k in range(8):
                        p.dma(s, qslot[:, k * 128:(k + 1) * 128],
                              q_my[k * 128:(k + 1) * 128, s_i * 128:(s_i + 1) * 128])
                    p.dma(s, obk_s[:], ownblk_d[:, s_i * 512:(s_i + 1) * 512])
                    p.dma(s, cmk_s[:], cmask_d[:, s_i * 513:(s_i + 1) * 513])
                    p.dma(s, slm_s[:], slmask_d[:, s_i * 256:(s_i + 1) * 256])
                    p.dma_wait_all(s)
            with nc.Block() as blk:
                @blk.vector
                def _(v, s_i=s_i):
                    v.tensor_scalar(nmc[:], cmk_s[:, 1:513], 1e9, -1e9,
                                    ALU.mult, ALU.add)
                    v.memset(impb[:], 0.0)
            for h in range(8):
                with nc.Block() as blk:
                    @blk.tensor
                    def _(t, s_i=s_i, h=h, b=b):
                        hb = (b * 8 + h) * 513
                        t.matmul(psc[0][:], qslot[:, h * 128:(h + 1) * 128],
                                 cks[:, hb + 1:hb + 513], start=True, stop=True)
                        t.matmul(psm1[:, 0:1], qslot[:, h * 128:(h + 1) * 128],
                                 cks[:, hb:hb + 1], start=True, stop=True)
                with nc.Block() as blk:
                    @blk.vector
                    def _(v, s_i=s_i, h=h):
                        cs = csb[:, h * 513:(h + 1) * 513]
                        v.tensor_single_scalar(cs[:, 0:1], psm1[:, 0:1], 0.12, ALU.mult)
                        v.scalar_tensor_tensor(wrk[:], psc[0][:], 0.12,
                                               cmk_s[:, 1:513], ALU.mult, ALU.mult)
                with nc.Block() as blk:
                    @blk.vector
                    def _(v, s_i=s_i, h=h):
                        cs = csb[:, h * 513:(h + 1) * 513]
                        v.tensor_add(cs[:, 1:513], wrk[:], nmc[:])
                        v.tensor_add(impb[:, 1:513], impb[:, 1:513], wrk[:])
            with nc.Block() as blk:
                @blk.vector
                def _(v):
                    v.tensor_single_scalar(impb[:, 1:513], impb[:, 1:513], 0.125, ALU.mult)
                    v.memset(impb[:, 0:1], -1000.0)
            with nc.Block() as blk:
                @blk.vector
                def _(v):
                    v.tensor_add(impb[:, 1:513], impb[:, 1:513], nmc[:])
                    v.reduce_max(nmx[:, 0:1], impb[:], AX.X, negate=True)
            with nc.Block() as blk:
                @blk.scalar
                def _(s):
                    s.activation(impb[:], impb[:], AF.Exp, bias=nmx[:, 0:1],
                                 accum_out=zrc[:, 0:1])
            with nc.Block() as blk:
                @blk.vector
                def _(v):
                    v.reciprocal(zrc[:, 0:1], zrc[:, 0:1])
            with nc.Block() as blk:
                @blk.vector
                def _(v):
                    v.tensor_scalar(isb[:], impb[:, 1:513], zrc[:, 0:1], None, ALU.mult)
                    v.tensor_copy(wrk[:], isb[:])
                    v.max(vals[:, 0:8], wrk[:])
                    v.match_replace(wrk[:], vals[:, 0:8], wrk[:], -1.0)
                    v.max(vals[:, 8:16], wrk[:])
                    v.match_replace(wrk[:], vals[:, 8:16], wrk[:], -1.0)
            with nc.Block() as blk:
                @blk.vector
                def _(v, s_i=s_i):
                    v.tensor_tensor(msel[:], wrk[:], isb[:], ALU.not_equal)
                    v.tensor_max(msel[:], msel[:], obk_s[:])
                    v.memset(zz16[:], 0.0)
                    v.tensor_scalar(gts[:], vals[:, 0:16], -1.0, 1.0, ALU.mult, ALU.add)
                    v.tensor_add(gts[:], gts[:], vals[:, 0:16])
                    v.tensor_tensor_scan(gts[:], gts[:], zz16[:], 1.0, ALU.mult, ALU.add)
            for h in (range(8) if "comp" not in _os.environ.get("P8_SKIP", "") else []):
                with nc.Block() as blk:
                    @blk.vector
                    def _(v, h=h):
                        cs = csb[:, h * 513:(h + 1) * 513]
                        v.reduce_max(nmx[:, 1:2], cs, AX.X, negate=True)
                with nc.Block() as blk:
                    @blk.scalar
                    def _(s, h=h):
                        s.activation(pcb[:], csb[:, h * 513:(h + 1) * 513], AF.Exp,
                                     bias=nmx[:, 1:2], accum_out=zrc[:, 1:2])
                with nc.Block() as blk:
                    @blk.vector
                    def _(v):
                        v.reciprocal(zrc[:, 1:2], zrc[:, 1:2])
                with nc.Block() as blk:
                    @blk.vector
                    def _(v):
                        v.tensor_scalar(pcb[:], pcb[:], zrc[:, 1:2], None, ALU.mult)
                with nc.Block() as blk:
                    @blk.sync
                    def _(s, s_i=s_i, h=h, b=b):
                        base = (b * 8 + h) * 513
                        for i in range(4):
                            p.dma(s, cvst[:, i * 128:(i + 1) * 128],
                                  cvtm_d[base + i * 128:base + (i + 1) * 128, :])
                        p.dma(s, cvst[0:1, 4 * 128:4 * 128 + 128],
                              cvtm_d[base + 512:base + 513, :])
                        p.dma_wait_all(s)
                with nc.Block() as blk:
                    @blk.tensor
                    def _(t, h=h):
                        for i in range(4):
                            t.transpose(pstv(i), pcb[:, i * 128:(i + 1) * 128], ident[:])
                        t.transpose(pstv(4)[0:1, :], pcb[:, 512:513], ident[:])
                with nc.Block() as blk:
                    @blk.vector
                    def _(v):
                        for i in range(4):
                            v.tensor_copy(ptb[:, i * 128:(i + 1) * 128], pstv(i))
                        v.tensor_copy(ptb[0:1, 512:640], pstv(4)[0:1, :])
                with nc.Block() as blk:
                    @blk.tensor
                    def _(t, h=h):
                        for i in range(4):
                            t.matmul(psav[:, 0:128], ptb[:, i * 128:(i + 1) * 128],
                                     cvst[:, i * 128:(i + 1) * 128],
                                     start=(i == 0), stop=False)
                        t.matmul(psav[:, 0:128], ptb[0:1, 512:640], cvst[0:1, 4 * 128:4 * 128 + 128],
                                 start=False, stop=True)
                with nc.Block() as blk:
                    @blk.vector
                    def _(v, s_i=s_i, h=h):
                        v.tensor_scalar(outb[:, h * 128:(h + 1) * 128], psav[:, 0:128],
                                        strat[:, s_i * 24 + 3 * h:s_i * 24 + 3 * h + 1], None, ALU.mult)

            kch = [p.sb(f"kch8_{s_i}_{i}", [128, 512], BF16, scope=sc) for i in range(2)]
            vchf = p.sb(f"vchf8_{s_i}", [128, EXT], BF16, scope=sc)
            ptbf = p.sb(f"ptbf8_{s_i}", [128, EXT], BF16, scope=sc, tdest=True)
            mmful = p.sb(f"mmf8_{s_i}", [128, EXT], BF16, scope=sc)
            nmful = p.sb(f"nmf8_{s_i}", [128, EXT], BF16, scope=sc)
            fsb = p.sb(f"fsb8_{s_i}", [128, EXT], scope=sc)
            pfb = p.sb(f"pfb8_{s_i}", [128, EXT], BF16, scope=sc)
            mca = p.sb(f"mca8_{s_i}", [128, 512], scope=sc)
            psav2 = p.ps([128, 512], scope=sc, name=f"psav2_{s_i}")

            for cc in range(NCC):
                with nc.Block() as blk:
                    @blk.vector
                    def _(v, s_i=s_i, cc=cc):
                        v.tensor_scalar(mca[:], dij[:], qbc[:, s_i:s_i + 1], float(-cc * 512),
                                        ALU.add, ALU.add)
                        v.tensor_single_scalar(mca[:], mca[:], 0.0, ALU.is_ge)
                        v.tensor_tensor(
                            mmful[:, cc * 512:(cc + 1) * 512].rearrange("p (w s) -> p w s", s=16),
                            mca[:].rearrange("p (w s) -> p w s", s=16),
                            msel[:, cc * 32:(cc + 1) * 32][:, :, None].broadcast_to([128, 32, 16]),
                            ALU.mult)
                        v.tensor_scalar(nmful[:, cc * 512:(cc + 1) * 512],
                                        mmful[:, cc * 512:(cc + 1) * 512], 1e9, -1e9,
                                        ALU.mult, ALU.add)

            kq = p.sem("kq8")
            tf = p.sem("tf8")
            vf = p.sem("vf8")
            for h in (range(8) if "fine" not in _os.environ.get("P8_SKIP", "") else []):
                kq0, tf0, vf0 = kq[1], tf[1], vf[1]
                with nc.Block() as blk:
                    @blk.sync
                    def _(s, s_i=s_i, h=h, b=b, vf0=vf0):
                        for cc in range(NCC):
                            if cc >= 2:
                                s.wait_ge(vf[0], vf0 + cc - 1)
                            p.dma(s, kch[cc % 2][:],
                                  kfull[h * 128:(h + 1) * 128,
                                        b * TG + cc * 512:b * TG + (cc + 1) * 512], q="kq8")

                    @blk.tensor
                    def _(t, s_i=s_i, h=h, kq0=kq0, vf0=vf0):
                        for cc in range(NCC):
                            t.wait_ge(kq[0], kq0 + 16 * (cc + 1))
                            if cc >= 2:
                                t.wait_ge(vf[0], vf0 + cc - 1)
                            inst = t.matmul(psc[cc % 2][:],
                                            qslot[:, h * 128:(h + 1) * 128],
                                            kch[cc % 2][:], start=True, stop=True)
                            inst.then_inc(tf[0], 1)

                    @blk.vector
                    def _(v, s_i=s_i, h=h, tf0=tf0):
                        for cc in range(NCC):
                            v.wait_ge(tf[0], tf0 + cc + 1)
                            v.scalar_tensor_tensor(fsb[:, cc * 512:(cc + 1) * 512], psc[cc % 2][:],
                                                   0.12, mmful[:, cc * 512:(cc + 1) * 512],
                                                   ALU.mult, ALU.mult)
                            inst = v.tensor_add(fsb[:, cc * 512:(cc + 1) * 512],
                                                fsb[:, cc * 512:(cc + 1) * 512],
                                                nmful[:, cc * 512:(cc + 1) * 512])
                            inst.then_inc(vf[0], 1)
                tf[1] += NCC
                vf[1] += NCC

                with nc.Block() as blk:
                    @blk.vector
                    def _(v):
                        v.reduce_max(nmx[:, 2:3], fsb[:], AX.X, negate=True)
                with nc.Block() as blk:
                    @blk.scalar
                    def _(s):
                        s.activation(pfb[:], fsb[:], AF.Exp, bias=nmx[:, 2:3], accum_out=zrc[:, 2:3])
                with nc.Block() as blk:
                    @blk.vector
                    def _(v):
                        v.reciprocal(zrc[:, 2:3], zrc[:, 2:3])
                with nc.Block() as blk:
                    @blk.vector
                    def _(v):
                        v.tensor_scalar(pfb[:], pfb[:], zrc[:, 2:3], None, ALU.mult)
                if "fine2" in _os.environ.get("P8_SKIP", ""):
                    continue
                NJT = EXT // 128
                with nc.Block() as blk:
                    @blk.sync
                    def _(s, s_i=s_i, h=h, b=b, NJT=NJT):
                        for jt in range(NJT):
                            p.dma(s, vchf[:, jt * 128:(jt + 1) * 128],
                                  vtm[b * TG + jt * 128:b * TG + (jt + 1) * 128,
                                      h * 128:(h + 1) * 128], q="vq8")
                        p.dma_wait_all(s, q="vq8")
                with nc.Block() as blk:
                    @blk.vector
                    def _(v):
                        v.tensor_copy(mca[:, 0:128], pfb[:, 0:128])
                with nc.Block() as blk:
                    @blk.sync
                    def _(s, NJT=NJT):
                        tq = p.sem("tq8")
                        for jt in range(NJT):
                            s.dma_start_transpose(ptbf[:, jt * 128:(jt + 1) * 128],
                                                  pfb[:, jt * 128:(jt + 1) * 128]).then_inc(tq[0], 16)
                            tq[1] += 16
                        s.wait_ge(tq[0], tq[1])
                if "fine3" not in _os.environ.get("P8_SKIP", ""):
                    with nc.Block() as blk:
                        @blk.tensor
                        def _(t, NJT=NJT):
                            for jt in range(NJT):
                                t.matmul(psav2[:, 0:128], ptbf[:, jt * 128:(jt + 1) * 128],
                                         vchf[:, jt * 128:(jt + 1) * 128],
                                         start=(jt == 0), stop=(jt == NJT - 1))
                with nc.Block() as blk:
                    @blk.vector
                    def _(v, s_i=s_i, h=h):
                        v.tensor_scalar(mca[:, 0:128], psav2[:, 0:128], gts[:, 15:16], None, ALU.mult)
                        v.scalar_tensor_tensor(outb[:, h * 128:(h + 1) * 128], mca[:, 0:128],
                                               strat[:, s_i * 24 + 3 * h + 1:s_i * 24 + 3 * h + 2],
                                               outb[:, h * 128:(h + 1) * 128], ALU.mult, ALU.add)

            kslb = p.sb(f"kslb{s_i}", [128, 8 * 256], BF16, scope=sc)
            vslb = p.sb(f"vslb{s_i}", [128, 2 * 1024], BF16, scope=sc)
            ssf = p.sb(f"ssf8_{s_i}", [128, 256], scope=sc)
            psl8 = p.ps([128, 512], scope=sc, name=f"psl8_{s_i}")
            with nc.Block() as blk:
                @blk.sync
                def _(s, s_i=s_i):
                    for k in range(8):
                        p.dma(s, kslb[:, k * 256:(k + 1) * 256],
                              ksl[k * 128:(k + 1) * 128, s_i * 256:(s_i + 1) * 256])
                    for i in range(2):
                        p.dma(s, vslb[:, i * 1024:(i + 1) * 1024],
                              vsl[s_i * 256 + i * 128:s_i * 256 + (i + 1) * 128, :])
                    p.dma_wait_all(s)
            for h in (range(8) if "slide" not in _os.environ.get("P8_SKIP", "") else []):
                with nc.Block() as blk:
                    @blk.tensor
                    def _(t, s_i=s_i, h=h):
                        t.matmul(psl8[:, 0:256], qslot[:, h * 128:(h + 1) * 128],
                                 kslb[:, h * 256:(h + 1) * 256], start=True, stop=True)
                with nc.Block() as blk:
                    @blk.vector
                    def _(v, s_i=s_i):
                        v.scalar_tensor_tensor(ssf[:], psl8[:, 0:256], 0.12,
                                               slm_s[:], ALU.mult, ALU.add)
                        v.reduce_max(nmx[:, 3:4], ssf[:], AX.X, negate=True)
                with nc.Block() as blk:
                    @blk.scalar
                    def _(s):
                        s.activation(pcb[:, 0:256], ssf[:], AF.Exp, bias=nmx[:, 3:4],
                                     accum_out=zrc[:, 3:4])
                with nc.Block() as blk:
                    @blk.vector
                    def _(v):
                        v.reciprocal(zrc[:, 3:4], zrc[:, 3:4])
                with nc.Block() as blk:
                    @blk.vector
                    def _(v):
                        v.tensor_scalar(pcb[:, 0:256], pcb[:, 0:256], zrc[:, 3:4], None, ALU.mult)
                with nc.Block() as blk:
                    @blk.tensor
                    def _(t):
                        t.transpose(pstv(0), pcb[:, 0:128], ident[:])
                        t.transpose(pstv(1), pcb[:, 128:256], ident[:])
                with nc.Block() as blk:
                    @blk.vector
                    def _(v):
                        v.tensor_copy(ptb[:, 0:128], pstv(0))
                        v.tensor_copy(ptb[:, 128:256], pstv(1))
                with nc.Block() as blk:
                    @blk.tensor
                    def _(t, h=h):
                        for i in range(2):
                            t.matmul(psav2[:, 0:128], ptb[:, i * 128:(i + 1) * 128],
                                     vslb[:, i * 1024 + h * 128:i * 1024 + (h + 1) * 128],
                                     start=(i == 0), stop=(i == 1))
                with nc.Block() as blk:
                    @blk.vector
                    def _(v, s_i=s_i, h=h):
                        v.scalar_tensor_tensor(outb[:, h * 128:(h + 1) * 128], psav2[:, 0:128],
                                               strat[:, s_i * 24 + 3 * h + 2:s_i * 24 + 3 * h + 3],
                                               outb[:, h * 128:(h + 1) * 128], ALU.mult, ALU.add)

            ofm = p.sb(f"ofm8_{s_i}", [128, 8 * 128], BF16, scope=sc)
            obb = p.sb(f"obb8_{s_i}", [128, 8 * 128], BF16, scope=sc)
            x7s = p.sb(f"x7s8_{s_i}", [128, 128], scope=sc)
            with nc.Block() as blk:
                @blk.vector
                def _(v):
                    v.tensor_copy(obb[:], outb[:])
            for half in range(2):
                with nc.Block() as blk:
                    @blk.tensor
                    def _(t, half=half):
                        for i in range(4):
                            hh = half * 4 + i
                            t.transpose(pstv(i), obb[:, hh * 128:(hh + 1) * 128], ident[:])
                with nc.Block() as blk:
                    @blk.vector
                    def _(v, half=half):
                        for i in range(4):
                            hh = half * 4 + i
                            v.tensor_copy(ofm[:, hh * 128:(hh + 1) * 128], pstv(i))
            for mt in range(8):
                with nc.Block() as blk:
                    @blk.tensor
                    def _(t, mt=mt):
                        for k in range(8):
                            t.matmul(psav2[:, 0:128], cpj[:, k * 1024 + mt * 128:k * 1024 + (mt + 1) * 128],
                                     ofm[:, k * 128:(k + 1) * 128],
                                     start=(k == 0), stop=(k == 7))

                    @blk.sync
                    def _(s, s_i=s_i, mt=mt):
                        p.dma(s, x7s[:], x7_d[mt * 128:(mt + 1) * 128, s_i * 128:(s_i + 1) * 128], q="x7q")
                        p.dma_wait_all(s, q="x7q")
                with nc.Block() as blk:
                    @blk.vector
                    def _(v):
                        v.scalar_tensor_tensor(x7s[:], psav2[:, 0:128], 1.0, x7s[:], ALU.mult, ALU.add)
                with nc.Block() as blk:
                    @blk.sync
                    def _(s, s_i=s_i, mt=mt):
                        p.dma(s, x8_o[mt * 128:(mt + 1) * 128, s_i * 128:(s_i + 1) * 128], x7s[:], q="st")
                        p.dma_wait_all(s, q="st")

    return p.done()




def host_phase8_inputs(inputs, r7s, x7s_std):
    qfull = np.zeros((1024, TGF), BF)
    kfull = np.zeros((1024, TGF), BF)
    vfull = np.zeros((1024, TGF), BF)
    strat_full = np.zeros((24, TGF), np.float32)
    for c in range(NC):
        for b_ in range(B):
            cols = slice(b_ * TG + c * TB, b_ * TG + (c + 1) * TB)
            qfull[:, cols] = r7s[c]["q_rot"][:, b_ * TB:(b_ + 1) * TB]
            kfull[:, cols] = r7s[c]["k_rot"][:, b_ * TB:(b_ + 1) * TB]
            vfull[:, cols] = r7s[c]["v_sc"][:, b_ * TB:(b_ + 1) * TB]
            strat_full[:, cols] = r7s[c]["strat"][:, b_ * TB:(b_ + 1) * TB]
    vtm = np.ascontiguousarray(vfull.T)
    ckf = np.zeros((2, 8, 513, 128), BF)
    cvf = np.zeros((2, 8, 513, 128), BF)
    mem = np.asarray(inputs["L1_attn_b_mem_kv"], np.float32)
    memk = np.ascontiguousarray(mem[0, :, 0, :])
    memv = np.ascontiguousarray(mem[1, :, 0, :].reshape(1, 1024))
    for c in range(NC):
        ck = r7s[c]["ck"]
        cv = r7s[c]["cv"]
        for h in range(8):
            for b_ in range(B):
                w0 = c * 64
                ckf[b_, h, 1 + w0:1 + w0 + 64, :] = ck[:, (h * 2 + b_) * 64:(h * 2 + b_) * 64 + 64].T
                cvf[b_, h, 1 + w0:1 + w0 + 64, :] = cv[:, (h * 2 + b_) * 64:(h * 2 + b_) * 64 + 64].T
    ckin = np.zeros((128, 16 * 513), BF)
    cvtm = np.zeros((16 * 513, 128), BF)
    for b_ in range(B):
        for h in range(8):
            s_ = b_ * 8 + h
            ckin[:, s_ * 513:(s_ + 1) * 513] = ckf[b_, h].T
            cvtm[s_ * 513:(s_ + 1) * 513, :] = cvf[b_, h]
    i_ = np.arange(128)[:, None]
    jj = np.arange(512)[None, :]
    dij = (i_ - jj).astype(np.float32)
    x7full = np.zeros((1024, TGF), np.float32)
    for c in range(NC):
        for b_ in range(B):
            x7full[:, b_ * TG + c * TB:b_ * TG + (c + 1) * TB] = x7s_std[c][:, b_ * TB:(b_ + 1) * TB]
    NEG = np.float32(-1e30)
    mrest = np.where((jj[:, :256][:, :128] * 0 == 0), 0.0, 0.0)
    j2 = np.arange(256)[None, :]
    slide_rest = np.where((j2 < 128) & (j2 >= i_) | (j2 >= 128) & (j2 - 128 <= i_), 0.0, NEG).astype(np.float32)
    slide_first = slide_rest.copy()
    slide_first[:, :128] = NEG
    maps = []
    for c in range(NC):
        q_my = np.zeros((1024, 2048), BF)
        ksl_my = np.zeros((1024, 16 * 256), BF)
        vsl_my = np.zeros((16 * 256, 1024), BF)
        strat_my = np.zeros((128, 16 * 24), np.float32)
        x7_my = np.zeros((1024, 2048), np.float32)
        qbcol = np.zeros((128, 16), np.float32)
        ownblk = np.zeros((128, 16 * 512), np.float32)
        cmask = np.zeros((128, 16 * 513), np.float32)
        slmask = np.zeros((128, 16 * 256), np.float32)
        for s_ in range(16):
            b_ = s_ // 8
            j = s_ % 8
            qt = 8 * j + c
            qb = qt * 128
            gcol = b_ * TG + qb
            q_my[:, s_ * 128:(s_ + 1) * 128] = qfull[:, gcol:gcol + 128]
            x7_my[:, s_ * 128:(s_ + 1) * 128] = x7full[:, gcol:gcol + 128]
            strat_my[:, s_ * 24:(s_ + 1) * 24] = strat_full[:, gcol:gcol + 128].T
            qbcol[:, s_] = qb
            if qb >= 128:
                ksl_my[:, s_ * 256:(s_ + 1) * 256] = kfull[:, gcol - 128:gcol + 128]
                vsl_my[s_ * 256:(s_ + 1) * 256, :] = vtm[gcol - 128:gcol + 128, :]
                slmask[:, s_ * 256:(s_ + 1) * 256] = slide_rest
            else:
                ksl_my[:, s_ * 256 + 128:(s_ + 1) * 256] = kfull[:, gcol:gcol + 128]
                vsl_my[s_ * 256 + 128:(s_ + 1) * 256, :] = vtm[gcol:gcol + 128, :]
                slmask[:, s_ * 256:(s_ + 1) * 256] = slide_first
            qpos = qb + i_[:, 0]
            ownblk[np.arange(128), s_ * 512 + (qpos // 16)] = 1.0
            wv = np.arange(512)[None, :]
            cm = ((wv + 1) * 16 - 1 < qpos[:, None]).astype(np.float32)
            cmask[:, s_ * 513] = 1.0
            cmask[:, s_ * 513 + 1:(s_ + 1) * 513] = cm
        maps.append({
            "q_my": np.ascontiguousarray(q_my),
            "kfull": kfull, "vtm": vtm,
            "ksl": np.ascontiguousarray(ksl_my), "vsl": np.ascontiguousarray(vsl_my),
            "ckin": ckin, "cvtm": cvtm,
            "strat_my": np.ascontiguousarray(strat_my),
            "x7_my": np.ascontiguousarray(x7_my),
            "qbcol": np.ascontiguousarray(qbcol),
            "ownblk": np.ascontiguousarray(ownblk),
            "dij": dij,
            "cmask": np.ascontiguousarray(cmask),
            "slmask": np.ascontiguousarray(slmask),
            "cprojT": hT(inputs["L1_attn_b_c_proj_w"]),
            "ident_i": IDENT128,
            "memk": memk,
            "memv": memv,
        })
    return maps


def host_phase8_out_to_std(inputs_unused, r8s):
    x8full = np.zeros((1024, TGF), np.float32)
    for c in range(NC):
        x8 = r8s[c]["x8_my"]
        for s_ in range(16):
            b_ = s_ // 8
            j = s_ % 8
            qb = (8 * j + c) * 128
            x8full[:, b_ * TG + qb:b_ * TG + qb + 128] = x8[:, s_ * 128:(s_ + 1) * 128]
    return [np.ascontiguousarray(np.concatenate(
        [x8full[:, b_ * TG + c * TB:b_ * TG + (c + 1) * TB] for b_ in range(B)], 1))
        for c in range(NC)]



def build_phase9():
    p = Phase("ph9")
    nc = p.nc
    x8 = p.din("x8_fm", [D, TL])
    fw = p.din("f1w", [D, D])
    fa1 = p.din("f1a1", [D, D])
    fa2 = p.din("f1a2", [D, D])
    fb = p.din("f1b", [128, 8])
    x9_o = p.dout("x9_fm", [D, TL])

    ssb = p.sb("ssb9", [128, 8 * TL])
    xscr = p.sb("xscr9", [128, 512])
    xnf = p.sb("xnf9", [128, 8 * TL], BF16)
    h1 = p.sb("h19", [128, 8 * TL], BF16)
    wbuf = p.sb("wbuf9", [128, 8 * 1024], BF16)
    fbc = p.sb("fbc9", [128, 8])
    ps4 = [p.ps([128, 512]) for _ in range(4)]

    with nc.Block() as blk:
        @blk.sync
        def _(s):
            for k in range(8):
                p.dma(s, ssb[:, k * TL:(k + 1) * TL], x8[k * 128:(k + 1) * 128, :])
            p.dma(s, fbc[:], fb)
            p.dma_wait_all(s)

    def apply9(v, c, psMu, psRs):
        for k in range(8):
            xs = ssb[:, k * TL + c * 512:k * TL + c * 512 + 512]
            xn = xnf[:, k * TL + c * 512:k * TL + c * 512 + 512]
            v.tensor_tensor(xscr[:], xs, psMu[:], ALU.subtract)
            v.tensor_tensor(xn, xscr[:], psRs[:], ALU.mult)

    with p.scope() as lnsc:
        emit_ln_apply(p, ssb, TL, {'v': apply9}, "ln9", scope=lnsc)

    for step, (wsrc, xin, xout) in enumerate(((fw, xnf, h1), (fa1, h1, xnf), (fa2, xnf, h1))):
        with nc.Block() as blk:
            @blk.gpsimd
            def _(g, wsrc=wsrc):
                for k in range(8):
                    p.dma(g, wbuf[:, k * 1024:(k + 1) * 1024], wsrc[k * 128:(k + 1) * 128, :], q="gld")
                p.dma_wait_all(g, q="gld")
        for mt in range(8):
            with nc.Block() as blk:
                @blk.tensor
                def _(t, xin=xin, mt=mt):
                    for ch in range(4):
                        for k in range(8):
                            t.matmul(ps4[ch][:], wbuf[:, k * 1024 + mt * 128:k * 1024 + (mt + 1) * 128],
                                     xin[:, k * TL + ch * 512:k * TL + ch * 512 + 512],
                                     start=(k == 0), stop=(k == 7))
            if step < 2:
                with nc.Block() as blk:
                    @blk.vector
                    def _(v, xout=xout, mt=mt):
                        for ch in range(4):
                            v.tensor_copy(xout[:, mt * TL + ch * 512:mt * TL + ch * 512 + 512], ps4[ch][:])
            else:
                with nc.Block() as blk:
                    @blk.scalar
                    def _(s, mt=mt):
                        for ch in range(4):
                            s.activation(h1[:, mt * TL + ch * 512:mt * TL + ch * 512 + 512],
                                         ps4[ch][:], AF.Tanh, bias=fbc[:, mt:mt + 1])
                with nc.Block() as blk:
                    @blk.vector
                    def _(v, mt=mt):
                        for ch in range(4):
                            sl = ssb[:, mt * TL + ch * 512:mt * TL + ch * 512 + 512]
                            v.tensor_tensor(sl, sl, h1[:, mt * TL + ch * 512:mt * TL + ch * 512 + 512], ALU.add)

    with nc.Block() as blk:
        @blk.sync
        def _(s):
            for k in range(8):
                p.dma(s, x9_o[k * 128:(k + 1) * 128, :], ssb[:, k * TL:(k + 1) * TL], q="st")
            p.dma_wait_all(s, q="st")

    return p.done()



def kernel(**inputs):
    cores = list(range(NC))
    x0 = f32a(inputs["L0_attn_a_x"])
    xs = [fm_slice(x0, c) for c in range(NC)]

    ph1 = build_phase1()
    r1 = run_bass_kernel_spmd(ph1.nc, host_phase1_inputs(inputs, xs), core_ids=cores).results
    ph2 = build_phase2()
    r2 = run_bass_kernel_spmd(ph2.nc, host_phase2_inputs(inputs, r1), core_ids=cores).results
    ph3 = build_phase3()
    m3 = host_phase3_inputs(inputs, r1, r2)
    for c in range(NC):
        m3[c]["xplus_fm"] = r1[c]["xplus_fm"]
    r3 = run_bass_kernel_spmd(ph3.nc, m3, core_ids=cores).results
    x3s = [np.asarray(r3[c]["x3_fm"]) for c in range(NC)]

    ph4a = build_phase4a()
    r4 = run_bass_kernel_spmd(ph4a.nc, host_phase4a_inputs(inputs, x3s), core_ids=cores).results
    x4s = [np.asarray(r4[c]["x4_fm"]) for c in range(NC)]
    ph4b = build_phase4b()
    r4b = run_bass_kernel_spmd(ph4b.nc, host_phase4b_inputs(inputs, x4s), core_ids=cores).results
    x5s = [np.asarray(r4b[c]["x5_fm"]) for c in range(NC)]
    xn5s = [np.asarray(r4b[c]["xn5a_fm"]) for c in range(NC)]
    xn5b = [np.asarray(r4b[c]["xn5a_bf"]) for c in range(NC)]

    ph5 = build_phase5()
    r5 = run_bass_kernel_spmd(ph5.nc, host_phase5_inputs(inputs, xn5b), core_ids=cores).results
    ph6 = build_phase6()
    r6 = run_bass_kernel_spmd(ph6.nc, host_phase6_inputs(inputs, r5), core_ids=cores).results
    yf = []
    for d in range(2):
        full = np.zeros((2048, TGF), BF)
        for c in range(NC):
            full[c * 256:(c + 1) * 256, :] = r6[c][f"y{d}"]
        yf.append(full)
    y1own, y2own = [], []
    for c in range(NC):
        t0 = c * TB
        o1 = np.concatenate([yf[0][:, b * TG + t0:b * TG + t0 + TB] for b in range(B)], 1)
        o2 = np.concatenate([yf[1][:, b * TG:(b + 1) * TG][:, ::-1][:, t0:t0 + TB]
                             for b in range(B)], 1)
        y1own.append(np.ascontiguousarray(o1))
        y2own.append(np.ascontiguousarray(o2))

    ph7 = build_phase7()
    r7 = run_bass_kernel_spmd(ph7.nc, host_phase7_inputs(inputs, x5s, xn5s, y1own, y2own),
                              core_ids=cores).results
    x7s_std = [np.asarray(r7[c]["x7_fm"]) for c in range(NC)]

    ph8 = build_phase8()
    r8 = run_bass_kernel_spmd(ph8.nc, host_phase8_inputs(inputs, r7, x7s_std),
                              core_ids=cores).results
    x8s = host_phase8_out_to_std(inputs, r8)

    ph9 = build_phase9()
    m9 = [{"x8_fm": x8s[c],
           "f1w": f32c(inputs["L1_ffn_weight"]),
           "f1a1": f32c(inputs["L1_ffn_aux_weight1"]),
           "f1a2": f32c(inputs["L1_ffn_aux_weight2"]),
           "f1b": vcols(inputs["L1_ffn_bias"])} for c in range(NC)]
    r9 = run_bass_kernel_spmd(ph9.nc, m9, core_ids=cores).results
    return unfm([r9[c]["x9_fm"] for c in range(NC)])
